# version 45
# baseline (speedup 1.0000x reference)
import math

import numpy as np
import ml_dtypes
import concourse.bass as bass
import concourse.mybir as mybir
from concourse import dve_ops
from concourse.bass_utils import run_bass_kernel_spmd

F32 = mybir.dt.float32
BF16 = mybir.dt.bfloat16
AF = mybir.ActivationFunctionType
ALU = mybir.AluOpType
AX = mybir.AxisListType

T = 2048
DM = 1024
NCH = 8
CW = 2052
NTB = 16
NQG = 4
NE = 8

PERS = {}
_pc = [0]
def _pers(name, nbytes):
    off = _pc[0]
    PERS[name] = off
    _pc[0] = (off + nbytes + 63) & ~63
    return off

O_XNP   = _pers("xnp", NCH * CW * 2)
O_BIASP = _pers("biasP", 24 * 4)
O_PROT  = _pers("prot", 128 * 2)
O_IDF   = _pers("idf", 128 * 4)
O_ONESR = _pers("onesr", 128 * 4)
O_SCAL  = _pers("scal", 16 * 4)
O_GSC   = _pers("gsc", 64 * 4)
O_ONESC = _pers("onesc", 64)
O_ONESRB= _pers("onesrb", 128 * 2)
O_GBCOL = _pers("gbcol", 64)
O_TMPA  = _pers("tmpA", 512 * 4)
O_J     = _pers("jtab", 512 * 4)
O_GW    = _pers("gw", 64 * 2)
ZB = (_pc[0] + 63) & ~63

Z = lambda r: ZB + r
O_XFM  = Z(0)
O_K    = Z(0)
O_MOE  = Z(0)
O_VLIN = Z(106496)
O_VFFN = Z(32768)
O_FFNW = Z(32768)
O_EWA  = Z(49152)
O_EWB  = Z(53248)
O_SELN = Z(57344)
O_EBSB = Z(59392)
O_WTBF = Z(61440)
O_WREP = Z(65536)
O_ASLAB= Z(32768)
O_OUTTM= Z(49152)
O_XTML = Z(57344)
O_ATTN = Z(73728)
O_STM  = Z(73728)
O_STR  = Z(81920)
O_XSQ  = Z(90112)
O_PTMP = Z(92160)
O_VROW = Z(73728)
O_PLIN = Z(73728)
O_MASK = Z(90112)
O_READS= Z(94208)
O_PFFN = Z(106496)
O_X    = Z(106496)
O_G    = Z(0)
O_WVT  = Z(90112)
O_EWC  = Z(139264)
O_EWD  = Z(143360)
O_XTMP = Z(122880)
O_XBG  = Z(131072)
O_COS  = Z(139264)
O_SIN  = Z(143360)
O_Q0   = Z(147456)
O_WRT  = Z(139264)
O_WKT  = Z(155648)
O_WQT  = Z(155648)
O_WWT  = Z(155648)
O_SGF  = Z(155648)
O_USTG = Z(155648)
O_FSC  = Z(172032)
ZEND = Z(173056)

SEMS = ["tn", "vc", "sc", "da", "db", "dc", "dd", "dy", "de", "dm",
        "dx0", "dx1", "ds0", "ds1", "dl0", "dl1", "dl2", "dl3", "yc"]

DELTAS = [-384 + 128 * i for i in range(16)]


class _B:

    def __init__(self, nc, who, eng, sems):
        self.nc = nc
        self.who = who
        self.eng = eng
        self.sems = sems
        self.cnt = {k: 0 for k in sems}
        self.hw = {k: 0 for k in sems}

    def _waits(self, deps):
        for d in deps:
            if d is None:
                continue
            s, v = d
            if v > self.hw[s]:
                self.eng.wait_ge(self.sems[s], v)
                self.hw[s] = v

    def op(self, owner, deps, inc, fn, amt=1):
        if self.who == owner:
            self._waits(deps)
            ins = fn(self.eng)
            if inc is not None:
                ins.then_inc(self.sems[inc], amt)
        if inc is not None:
            self.cnt[inc] += amt
            return (inc, self.cnt[inc])
        return None

    def t(self, deps, fn, inc="tn"):
        return self.op("t", deps, inc, fn)

    def v(self, deps, fn, inc="vc"):
        return self.op("v", deps, inc, fn)

    def s(self, deps, fn, inc="sc"):
        return self.op("s", deps, inc, fn)

    def y(self, deps, fn, inc="yc"):
        return self.op("q", deps, inc, fn)

    def dma(self, owner, deps, sem, fn):
        return self.op(owner, deps, sem, fn, amt=16)

    def wait(self, owner, deps):
        if self.who == owner:
            self._waits(deps)


class _H:
    pass


def _build_nc():
    nc = bass.Bass("TRN2", target_bir_lowering=False)
    H = _H()
    H.nc = nc

    def din(name, shape, dt=F32):
        return nc.dram_tensor(name, list(shape), dt, kind="ExternalInput")

    H.d_x = din("x", [T, DM])
    H.d_wq = din("wqT", [DM, DM])
    H.d_wk = din("wkT", [DM, DM])
    H.d_wv = din("wvT", [DM, DM])
    H.d_ww = din("wwT", [DM, DM])
    H.d_wr = din("wrT", [DM, DM])
    H.d_wf = din("wfT", [DM, DM])
    H.d_ew = din("ewT", [NE, 4, NE, 2, 128, 128])
    H.d_eb = din("eb", [NE, DM])
    H.d_biasP = din("biasP", [128, 24])
    H.d_gb = din("gbcol", [NE, 1])
    H.d_vrow = din("vrow", [1, DM])
    H.d_gw = din("gwT", [128, 64])
    H.d_cos = din("cosT", [128, T])
    H.d_sin = din("sinT", [128, T])
    H.d_prot = din("prot", [128, 128])
    H.d_idf = din("idf", [128, 128])
    H.d_onesr = din("onesr", [64, 128])
    H.d_onesc = din("onesc", [128, 1])
    H.d_onesrb = din("onesrb", [64, 128])
    H.d_j = din("jtab", [128, 512])
    H.d_mask = din("maskT", [128, 4, 512])
    H.d_seln = din("seln", [NE, DM])
    H.d_dec = din("decay", [1, 1])
    H.d_la = din("log_alpha", [1, 1])
    H.d_out = nc.dram_tensor("out", [T, DM], F32, kind="ExternalOutput")

    H.arena = nc.alloc_sbuf_tensor("arena", [128, ZEND], mybir.dt.uint8)
    A0 = nc.lookup_mloc(H.arena).addr

    def sb(name, shape, dt, off):
        return nc.alloc_sbuf_tensor_at(name, list(shape), dt, offset=A0 + off)

    H.xnp = sb("xnp", [128, NCH, CW], BF16, O_XNP)
    H.biasP = sb("biasP_s", [128, 24], F32, O_BIASP)
    H.prot = sb("prot_s", [128, 128], BF16, O_PROT)
    H.idf = sb("idf_s", [128, 128], F32, O_IDF)
    H.onesr = sb("onesr_s", [64, 128], F32, O_ONESR)
    H.scal = sb("scal", [128, 16], F32, O_SCAL)
    H.gsc = sb("gsc", [128, 64], F32, O_GSC)
    H.onesc = sb("onesc_s", [128, 1], BF16, O_ONESC)
    H.onesrb = sb("onesrb_s", [64, 128], BF16, O_ONESRB)
    H.gbcol = sb("gbcol_s", [NE, 1], F32, O_GBCOL)
    H.tmpA = sb("tmpA", [128, 512], F32, O_TMPA)
    H.jtab = sb("jtab_s", [128, 512], F32, O_J)
    H.gw = sb("gw_s", [128, 64], BF16, O_GW)
    H.seln = sb("seln_s", [NE, DM], BF16, O_SELN)
    H.vrow = sb("vrow_s", [1, DM], BF16, O_VROW)

    H.xfm = sb("xfm", [128, NCH, T], F32, O_XFM)
    H.k_ = sb("k_", [128, NCH, T], BF16, O_K)
    H.moe = sb("moe", [128, NCH, T], BF16, O_MOE)
    H.vlin = sb("vlin", [128, NTB, DM], BF16, O_VLIN)
    H.vffn = sb("vffn", [128, NTB, NE, 160], BF16, O_VFFN)
    H.ffnw = sb("ffnw", [128, NCH, DM], BF16, O_FFNW)
    H.ewa = sb("ewa", [128, 16, 128], BF16, O_EWA)
    H.aslab = sb("aslab", [128, 16, 512], BF16, O_ASLAB)
    H.outtm = sb("outtm", [128, 2, DM], F32, O_OUTTM)
    H.xtml = sb("xtml", [128, 4, DM], F32, O_XTML)
    H.attn = sb("attn", [128, NCH, T], BF16, O_ATTN)
    H.stm = sb("stm", [1, T], F32, O_STM)
    H.str_ = sb("str", [1, T], F32, O_STR)
    H.xsq = sb("xsq", [128, 2, 512], BF16, O_XSQ)
    H.ptmp = sb("ptmp", [128, 512], F32, O_PTMP)
    H.plin = sb("plin", [128, 16, 512], BF16, O_PLIN)
    H.mask = sb("mask", [128, 4, 512], BF16, O_MASK)
    H.reads = sb("reads", [128, NCH, 512], BF16, O_READS)
    H.wvt = sb("wvt", [128, NCH, DM], BF16, O_WVT)
    H.xtmp = sb("xtmp", [128, 2, DM], F32, O_XTMP)
    H.xbg = sb("xbg", [128, NCH, 512], BF16, O_XBG)
    H.pffn = sb("pffn", [128, 2, 16, 512], BF16, O_PFFN)
    H.xsl = sb("xsl", [128, 2, NE, 2, 512], BF16, O_X)
    H.g_ = sb("g_", [128, NCH, T], BF16, O_G)
    H.ustg = sb("ustg", [128, NCH, 512], F32, O_USTG)
    H.cos = sb("cos", [128, T], BF16, O_COS)
    H.sin = sb("sin", [128, T], BF16, O_SIN)
    H.q0 = sb("q0", [128, NCH, 512], BF16, O_Q0)
    H.k0 = sb("k0", [128, 2, T], BF16, O_Q0)
    H.ewb = sb("ewb", [128, 16, 128], BF16, O_EWB)
    H.ewc = sb("ewc", [128, 16, 128], BF16, O_EWC)
    H.ewd = sb("ewd", [128, 16, 128], BF16, O_EWD)
    H.wrt = sb("wrt", [128, NCH, DM], BF16, O_WRT)
    H.wkt = sb("wkt", [128, NCH, DM], BF16, O_WKT)
    H.wqt = sb("wqt", [128, NCH, DM], BF16, O_WQT)
    H.wtbf = sb("wtbf", [NE, T], BF16, O_WTBF)
    H.ebsb = sb("ebsb", [NE, DM], BF16, O_EBSB)
    H.wrep = sb("wrep", [128, NE, 512], BF16, O_WREP)
    H.wwt = sb("wwt", [128, NCH, DM], BF16, O_WWT)
    H.sgf = sb("sgf", [128, NCH, 512], F32, O_SGF)
    H.fsc = sb("fsc", [128, 512], BF16, O_FSC)

    H.psb = nc.alloc_psum_tensor("psb", [128, 2048], F32)
    H.ps = [None, None,
            nc.alloc_psum_tensor("ps2", [128, 1024], F32),
            nc.alloc_psum_tensor("ps3", [128, 1024], F32)]

    H.sems = {name: nc.alloc_semaphore(name) for name in SEMS}

    with nc.Block() as block:

        @block.sync
        def _(e):
            _program(_B(nc, "y", e, H.sems), H)

        @block.gpsimd
        def _(e):
            _program(_B(nc, "q", e, H.sems), H)

        @block.tensor
        def _(e):
            _program(_B(nc, "t", e, H.sems), H)

        @block.vector
        def _(e):
            _program(_B(nc, "v", e, H.sems), H)

        @block.scalar
        def _(e):
            _program(_B(nc, "s", e, H.sems), H)

    return nc, H


def _bank(H, b):
    if b < 4:
        return H.psb[:, b * 512:(b + 1) * 512]
    return H.ps[b // 2][:, (b % 2) * 512:(b % 2 + 1) * 512]


def _fence(b: _B):
    toks = [(sname, b.cnt[sname]) for sname in ("tn", "vc", "sc")]
    for who in ("t", "v", "s", "q", "y"):
        b.wait(who, toks)


def _program(b: _B, H):
    nc = H.nc
    bankF = [None] * 8
    cells = H.scal

    for dst, src in [(H.biasP[:], H.d_biasP.ap()), (H.idf[:], H.d_idf.ap()),
                     (H.onesr[:], H.d_onesr.ap()), (H.gbcol[:], H.d_gb.ap()),
                     (H.jtab[:], H.d_j.ap()),
                     (cells[0:1, 14:15], H.d_dec.ap()),
                     (cells[0:1, 15:16], H.d_la.ap())]:
        tok_dy = b.dma("y", [], "dy", lambda e, d=dst, s=src: e.dma_start(d, s))
    for dst, src in [(H.cos[:], H.d_cos.ap()), (H.sin[:], H.d_sin.ap()),
                     (H.prot[:], H.d_prot.ap()), (H.onesc[:], H.d_onesc.ap()),
                     (H.onesrb[:], H.d_onesrb.ap()), (H.gw[:], H.d_gw.ap())]:
        tok_dd = b.dma("q", [], "dd", lambda e, d=dst, s=src: e.dma_start(d, s))

    b.v([], lambda e: e.memset(cells[:, 0:1], -1.0))
    b.v([], lambda e: e.memset(cells[:, 1:2], 0.125))
    b.v([], lambda e: e.memset(cells[:, 2:3], -8.0))
    b.v([], lambda e: e.memset(cells[:, 3:4], 1e-5))
    b.v([], lambda e: e.memset(cells[:, 4:5], -0.5))
    b.v([], lambda e: e.memset(cells[:, 7:8], 0.0))
    tok_cells = b.v([], lambda e: e.memset(H.xnp[:, :, 0:1], 0.0))

    c = lambda i: cells[0:1, i:i + 1]
    tok_z = b.s([tok_dy, tok_cells],
                lambda e: e.activation(c(8), c(14), AF.Exp, scale=cells[0:1, 0:1]))
    tok_acc = b.v([tok_z], lambda e: e.memset(c(9), -1.0 / 12.0))
    for k in range(11, 0, -1):
        ck = ((-1.0) ** (k + 1)) / k
        tok_acc = b.v([tok_acc],
                      lambda e, ck=ck: e.tensor_scalar(c(9), c(9), c(8), ck,
                                                       ALU.mult, ALU.add))
    tok_ls = b.v([tok_acc], lambda e: e.tensor_scalar(c(10), c(9), c(8), None,
                                                      ALU.mult))
    tok_u = b.v([tok_z], lambda e: e.tensor_scalar(c(11), c(8), 1.0, None,
                                                   ALU.add))
    tok_ln = b.s([tok_u], lambda e: e.activation(c(11), c(11), AF.Ln))
    tok_w = b.v([tok_dy], lambda e: e.tensor_scalar(c(12), c(14), -2.0, 1e9,
                                                    ALU.add, ALU.mult))
    tok_w = b.v([tok_w], lambda e: e.tensor_scalar(c(12), c(12), 0.0, 1.0,
                                                   ALU.max, ALU.min))
    tok_d = b.v([tok_ls, tok_ln], lambda e: e.tensor_tensor(c(13), c(10), c(11),
                                                            ALU.subtract))
    tok_L = b.v([tok_d, tok_w],
                lambda e: e.scalar_tensor_tensor(c(10), c(13), c(12), c(11),
                                                 ALU.mult, ALU.add))
    tok_L = b.v([tok_L], lambda e: e.tensor_scalar(c(10), c(10), -1.0, None,
                                                   ALU.mult))
    tok_al = b.s([tok_dy, tok_L], lambda e: e.activation(c(11), c(15), AF.Exp))
    tok_r1 = b.t([tok_L], lambda e: e.matmul(_bank(H, 6)[:, 0:1], H.onesr[0:1, :],
                                             c(10), start=True, stop=True))
    tok_r2 = b.t([tok_al], lambda e: e.matmul(_bank(H, 7)[:, 0:1], H.onesr[0:1, :],
                                              c(11), start=True, stop=True))
    tok_lg = b.v([tok_r1], lambda e: e.tensor_copy(H.scal[:, 5:6],
                                                   _bank(H, 6)[:, 0:1]))
    tok_alv = b.v([tok_r2], lambda e: e.tensor_copy(H.scal[:, 6:7],
                                                    _bank(H, 7)[:, 0:1]))
    bankF[6] = tok_lg
    bankF[7] = tok_alv

    tk_xT = {}
    tk_xfm = {}
    tk_prev_stats = None
    dxs = ["dx0", "dx1"]
    for tb in range(NTB):
        g = tb // 4
        par = tb % 2
        deps = [tk_xT.get(tb - 2)]
        tok_x = b.dma("y" if par == 0 else "q", deps, dxs[par],
                      lambda e, tb=tb, par=par: e.dma_start(
                          H.xtmp[:, par, :], H.d_x.ap()[tb * 128:(tb + 1) * 128, :]))
        for ci in range(NCH):
            bk = 4 + (ci % 4)
            tok_tp = b.t([tok_x, bankF[bk], tok_dy],
                         lambda e, tb=tb, par=par, ci=ci, bk=bk: e.transpose(
                             _bank(H, bk)[:, 0:128],
                             H.xtmp[:, par, ci * 128:(ci + 1) * 128], H.idf[:]))
            if ci == NCH - 1:
                tk_xT[tb] = tok_tp
            tok_e = b.s([tok_tp],
                        lambda e, tb=tb, ci=ci, bk=bk: e.copy(
                            H.xfm[:, ci, tb * 128:(tb + 1) * 128],
                            _bank(H, bk)[:, 0:128]))
            bankF[bk] = tok_e
            tk_xfm[(tb, ci)] = tok_e

        if tb % 4 == 3:
            tk_xb = {}
            tk_sq = {}
            tok_sum = None
            tok_ssq = None
            for ci in range(NCH):
                dep_f = [tk_xfm[(g * 4 + j, ci)] for j in range(4)]
                dep_f.append(tk_prev_stats)
                tk_xb[ci] = b.v(dep_f, lambda e, ci=ci, g=g: e.tensor_copy(
                    H.xbg[:, ci, :], H.xfm[:, ci, g * 512:(g + 1) * 512]))
                tk_sq[ci] = b.v([tk_xb[ci], tk_sq.get(ci - 2)],
                                lambda e, ci=ci: e.tensor_tensor(
                                    H.xsq[:, ci % 2, :], H.xbg[:, ci, :],
                                    H.xbg[:, ci, :], ALU.mult))
                tok_sum = b.t([tk_xb[ci]] + ([bankF[0], tok_dd] if ci == 0 else []),
                              lambda e, ci=ci: e.matmul(
                                  _bank(H, 0)[0:1, :], H.onesc[:],
                                  H.xbg[:, ci, :], start=(ci == 0),
                                  stop=(ci == NCH - 1)))
                tok_ssq = b.t([tk_sq[ci]] + ([bankF[1]] if ci == 0 else []),
                              lambda e, ci=ci: e.matmul(
                                  _bank(H, 1)[0:1, :], H.onesc[:],
                                  H.xsq[:, ci % 2, :], start=(ci == 0),
                                  stop=(ci == NCH - 1)))
                tk_sq[ci] = tok_ssq
            gs = slice(g * 512, (g + 1) * 512)
            tok_m = b.v([tok_sum], lambda e, gs=gs: e.tensor_scalar(
                H.stm[0:1, gs], _bank(H, 0)[0:1, :], 1.0 / DM, None, ALU.mult))
            bankF[0] = tok_m
            tok_e2 = b.v([tok_ssq], lambda e: e.tensor_scalar(
                H.ptmp[0:1, :], _bank(H, 1)[0:1, :], 1.0 / DM, None, ALU.mult))
            bankF[1] = tok_e2
            tok_m2 = b.v([tok_m], lambda e, gs=gs: e.tensor_tensor(
                H.tmpA[0:1, 0:512], H.stm[0:1, gs], H.stm[0:1, gs], ALU.mult))
            tok_var = b.v([tok_e2, tok_m2], lambda e: e.tensor_tensor(
                H.ptmp[0:1, :], H.ptmp[0:1, :], H.tmpA[0:1, 0:512],
                ALU.subtract))
            tok_ln2 = b.s([tok_var], lambda e: e.activation(
                H.ptmp[0:1, :], H.ptmp[0:1, :], AF.Ln, bias=cells[0:1, 3:4]))
            tok_r = b.s([tok_ln2], lambda e, gs=gs: e.activation(
                H.str_[0:1, gs], H.ptmp[0:1, :], AF.Exp, scale=cells[0:1, 4:5]))
            tok_mr = b.t([tok_m, bankF[2]], lambda e, gs=gs: e.matmul(
                _bank(H, 2), H.onesr[0:1, :], H.stm[0:1, gs], start=True, stop=True))
            tok_rr = b.t([tok_r, bankF[3]], lambda e, gs=gs: e.matmul(
                _bank(H, 3), H.onesr[0:1, :], H.str_[0:1, gs], start=True, stop=True))
            tok_u2 = tok_r
            for ci in range(NCH):
                tok_u2 = b.v([tok_mr, tok_u2], lambda e, ci=ci, gs=gs:
                             e.tensor_tensor(H.ptmp[:], H.xfm[:, ci, gs],
                                             _bank(H, 2), ALU.subtract))
                tok_u2 = b.v([tok_rr, tok_u2], lambda e, ci=ci, g=g:
                             e.tensor_tensor(
                                 H.xnp[:, ci, 1 + g * 512:1 + (g + 1) * 512],
                                 H.ptmp[:], _bank(H, 3), ALU.mult))
            bankF[2] = tok_u2
            bankF[3] = tok_u2
            tk_prev_stats = tok_ssq
    tok_xnp = tok_u2
    wk_src = H.d_wk.ap().rearrange("(c p) j -> p c j", p=128)
    tok_wk = b.dma("q", [], "da", lambda e: e.dma_start(H.wkt[:], wk_src))
    _fence(b)

    tok_lastrep = ("tn", b.cnt["tn"])
    tok_vrow = b.dma("q", [tok_lastrep], "dd",
                     lambda e: e.dma_start(H.vrow[:], H.d_vrow.ap()))
    wv_src = H.d_wv.ap().rearrange("(c p) j -> p c j", p=128)
    tok_wv = b.dma("q", [tok_lastrep], "db",
                   lambda e: e.dma_start(H.wvt[:], wv_src))
    tok_vones = b.v([tok_xnp], lambda e: e.memset(H.vffn[:, :, :, 64:65], 1.0))
    tok_vones = b.v([tok_vones],
                    lambda e: e.memset(H.vffn[:, :, :, 65:96], 0.0))

    def xnp_tok(ci, lo, n):
        return H.xnp[:, ci, lo:lo + n]

    tk_k0 = {}
    tk_rope_prev = None
    for co in range(NCH):
        for gg in range(NQG):
            bk = (co * NQG + gg) % 4
            for ci in range(NCH):
                deps = [tok_wk, tok_xnp] if ci == 0 else []
                if ci == 0:
                    deps.append(bankF[bk])
                b.t(deps, lambda e, co=co, gg=gg, ci=ci, bk=bk: e.matmul(
                    _bank(H, bk), H.wkt[:, ci, co * 128:(co + 1) * 128],
                    xnp_tok(ci, 1 + gg * 512, 512),
                    start=(ci == 0), stop=(ci == NCH - 1)),
                    inc=None if ci < NCH - 1 else "tn")
            tok_mm = ("tn", b.cnt["tn"])
            tok_ev = b.s([tok_mm, tk_k0.get((co - 2, NQG - 1))],
                         lambda e, co=co, gg=gg, bk=bk: e.activation(
                H.k0[:, co % 2, gg * 512:(gg + 1) * 512], _bank(H, bk),
                AF.Identity, bias=H.biasP[:, 8 + co:9 + co]))
            bankF[bk] = tok_ev
            tk_k0[(co, gg)] = tok_ev
        for gg in range(NQG):
            bk = 4 + gg
            tok_rot = b.t([tk_k0[(co, gg)], bankF[bk], tok_dd],
                          lambda e, co=co, gg=gg, bk=bk: e.matmul(
                              _bank(H, bk), H.prot[:],
                              H.k0[:, co % 2, gg * 512:(gg + 1) * 512],
                              start=True, stop=True))
            gsl = slice(gg * 512, (gg + 1) * 512)
            tok_ta = b.v([tok_rot, tok_dd, tk_rope_prev],
                         lambda e, gsl=gsl, bk=bk: e.tensor_tensor(
                H.tmpA[:, 0:512], _bank(H, bk), H.sin[:, gsl], ALU.mult))
            bankF[bk] = tok_ta
            tok_kc = b.v([tok_ta], lambda e, co=co, gg=gg, gsl=gsl:
                         e.tensor_tensor(H.k_[:, co, gsl],
                                         H.k0[:, co % 2, gsl],
                                         H.cos[:, gsl], ALU.mult))
            tok_kc = b.v([tok_kc], lambda e, co=co, gsl=gsl: e.tensor_tensor(
                H.k_[:, co, gsl], H.tmpA[:, 0:512], H.k_[:, co, gsl], ALU.add))
            tk_k0[(co, gg)] = tok_kc
            tk_rope_prev = tok_kc

    tok_klast = tok_kc
    wq_src = H.d_wq.ap().rearrange("(c p) j -> p c j", p=128)
    tok_wq = b.dma("q", [tok_klast], "da",
                   lambda e: e.dma_start(H.wqt[:], wq_src))

    _fence(b)

    tk_q = {}
    tk_pvkt = {}
    tk_ps_all = {}
    prev = None
    tok_attn = None

    def qproj(g):
        gsl = slice(g * 512, (g + 1) * 512)
        for co in range(NCH):
            bk = co % 4
            for ci in range(NCH):
                deps = [tok_wq, tk_scolast.get(co)] if ci == 0 else []
                if ci == 0:
                    deps.append(bankF[bk])
                b.t(deps, lambda e, co=co, ci=ci, bk=bk, g=g: e.matmul(
                    _bank(H, bk), H.wqt[:, ci, co * 128:(co + 1) * 128],
                    xnp_tok(ci, 1 + g * 512, 512),
                    start=(ci == 0), stop=(ci == NCH - 1)),
                    inc=None if ci < NCH - 1 else "tn")
            tok_mm = ("tn", b.cnt["tn"])
            tok_ev = b.v([tok_mm], lambda e, co=co, bk=bk: e.tensor_scalar(
                H.q0[:, co, :], _bank(H, bk), H.biasP[:, co:co + 1], None,
                ALU.add))
            bankF[bk] = tok_ev
            bkr = 1 + 2 * (co % 2)
            tok_rot = b.t([tok_ev, bankF[bkr]],
                          lambda e, co=co, bkr=bkr: e.matmul(
                              _bank(H, bkr), H.prot[:], H.q0[:, co, :],
                              start=True, stop=True))
            tok_ta = b.v([tok_rot, tk_rope_prev[0]],
                         lambda e, bkr=bkr, gsl=gsl: e.tensor_tensor(
                H.tmpA[:, 0:512], _bank(H, bkr), H.sin[:, gsl], ALU.mult))
            bankF[bkr] = tok_ta
            tok_qc = b.v([tok_ta], lambda e, co=co, gsl=gsl: e.tensor_tensor(
                H.q0[:, co, :], H.q0[:, co, :], H.cos[:, gsl], ALU.mult))
            tok_qc = b.v([tok_qc], lambda e, co=co: e.tensor_tensor(
                H.q0[:, co, :], H.tmpA[:, 0:512], H.q0[:, co, :], ALU.add))
            tk_q[co] = tok_qc
            tk_rope_prev[0] = tok_qc

    def pv_kt(pp, kt, deps_extra=()):
        pg, php = pp
        pvt = H.ps[2 + pairno[pp] % 2]
        pvb = 4 + 2 * (pairno[pp] % 2)
        for h2 in range(2):
            deps = [tk_ps_all[(pp, kt)], tk_vtb.get(kt), tok_vones] \
                + list(deps_extra)
            if kt == 0:
                deps += [bankF[pvb + h2]]
            if h2 == 0:
                vs = H.vffn[:, kt, php, 0:128]
                ot = pvt[:, 0:512]
            else:
                vs = H.vffn[:, kt, php, 32:160]
                ot = pvt[:, 512:1024]
            b.t(deps, lambda e, kt=kt, vs=vs, ot=ot, h2=h2, php=php: e.matmul(
                ot, vs, H.pffn[:, h2, kt, :],
                start=(kt == 0), stop=(kt == NTB - 1)),
                inc=None if h2 == 0 else "tn")
        tk_pvkt[(pp, kt)] = ("tn", b.cnt["tn"])

    def norm_recips(pp):
        tok_pvl = tk_pvkt[(pp, NTB - 1)]
        pvt = H.ps[2 + pairno[pp] % 2]
        def _r1(e, pvt=pvt):
            with H.nc.allow_low_precision("softmax denom applied to bf16 attn"):
                return e.reciprocal(H.fsc[0:1, :], pvt[64:65, 0:512])

        def _r2(e, pvt=pvt):
            with H.nc.allow_low_precision("softmax denom applied to bf16 attn"):
                return e.reciprocal(H.fsc[32:33, :], pvt[32:33, 512:1024])
        tok_re = b.v([tok_pvl, tk_rope_prev[0]], _r1)
        tok_ro = b.v([tok_re], _r2)
        return (pp, tok_re, tok_ro)

    def norm_tail(pp, tok_re, tok_ro):
        pg, php = pp
        gsl = slice(pg * 512, (pg + 1) * 512)
        pvt = H.ps[2 + pairno[pp] % 2]
        pvb = 4 + 2 * (pairno[pp] % 2)
        tok_fe = b.t([tok_re, bankF[1]], lambda e: e.matmul(
            _bank(H, 1), H.onesrb[0:1, :], H.fsc[0:1, :],
            start=True, stop=True))
        tok_fo = b.t([tok_ro, bankF[3]], lambda e: e.matmul(
            _bank(H, 3), H.onesrb[32:33, :], H.fsc[32:33, :],
            start=True, stop=True))
        tok_c1 = b.v([tok_fe, tok_fo], lambda e: e.tensor_copy(
            H.fsc[:], _bank(H, 1)))
        bankF[1] = tok_c1
        tok_ae = b.v([tok_c1], lambda e, php=php, gsl=gsl, pvt=pvt:
                     e.tensor_tensor(H.attn[0:64, php, gsl],
                                     pvt[0:64, 0:512],
                                     H.fsc[0:64, :], ALU.mult))
        tok_c2 = b.v([tok_ae, tok_fo], lambda e: e.tensor_copy(
            H.fsc[:], _bank(H, 3)))
        bankF[3] = tok_c2
        tok_ao = b.v([tok_c2], lambda e, php=php, gsl=gsl, pvt=pvt:
                     e.tensor_tensor(H.attn[64:128, php, gsl],
                                     pvt[64:128, 512:1024],
                                     H.fsc[64:128, :], ALU.mult))
        bankF[pvb] = bankF[pvb + 1] = tok_ao
        tk_rope_prev[0] = tok_ao
        return tok_ao

    tk_rope_prev = [tk_rope_prev]
    tk_scolast = {}
    tk_vtb = {}
    pairno = {}
    pend = [None]
    vq = [(tb, dg) for tb in range(NTB) for dg in range(2)]
    vqi = [0]

    def vgroup(bk):
        tb, dg = vq[vqi[0]]
        vqi[0] += 1
        for ci in range(NCH):
            deps = [tok_wv, tok_vones, tok_vrow, bankF[bk]] if ci == 0 else []
            b.t(deps, lambda e, tb=tb, dg=dg, ci=ci, bk=bk: e.matmul(
                _bank(H, bk), xnp_tok(ci, 1 + tb * 128, 128),
                H.wvt[:, ci, dg * 512:(dg + 1) * 512],
                start=(ci == 0), stop=False), inc=None)
        b.t([], lambda e, dg=dg, bk=bk: e.matmul(
            _bank(H, bk), H.onesrb[0:1, :], H.vrow[0:1, dg * 512:(dg + 1) * 512],
            start=False, stop=True))
        tok_mm = ("tn", b.cnt["tn"])
        src = _bank(H, bk).rearrange("p (h x) -> p h x", h=4)
        tok_e1 = b.v([tok_mm], lambda e, tb=tb, dg=dg, src=src:
                     e.tensor_copy(H.vffn[:, tb, 4 * dg:4 * dg + 4, 0:64],
                                   src[:, :, 0:64]))
        tok_e2 = b.v([tok_e1], lambda e, tb=tb, dg=dg, src=src:
                     e.tensor_copy(H.vffn[:, tb, 4 * dg:4 * dg + 4, 96:160],
                                   src[:, :, 64:128]))
        bankF[bk] = tok_e2
        if dg == 1:
            tk_vtb[tb] = tok_e2

    npair = 0
    qproj(0)
    for g in range(NQG):
        for hp in range(NCH):
            pp = (g, hp)
            pairno[pp] = npair
            for kt in range(NTB):
                if kt == 6 and pend[0] is not None:
                    tok_attn = norm_tail(*pend[0])
                    pend[0] = None
                sb_e = kt % 2
                sb_o = 2 + kt % 2
                hf = (kt % 2) * 512
                deps_e = [tk_q[hp], tk_k0.get((hp, NQG - 1)), bankF[sb_e]]
                b.t(deps_e, lambda e, hp=hp, kt=kt, hf=hf: e.matmul(
                    H.psb[:, hf:hf + 512],
                    H.k_[0:64, hp, kt * 128:(kt + 1) * 128],
                    H.q0[0:64, hp, :], start=True, stop=True), inc=None)
                b.t([bankF[sb_o]], lambda e, hp=hp, kt=kt, hf=hf: e.matmul(
                    H.psb[:, 1024 + hf:1024 + hf + 512],
                    H.k_[64:128, hp, kt * 128:(kt + 1) * 128],
                    H.q0[64:128, hp, :], start=True, stop=True,
                    tile_position=(64, 0)), inc="tn")
                tok_sco = ("tn", b.cnt["tn"])
                if vqi[0] < len(vq) and npair < 2:
                    vgroup(6 + vqi[0] % 2)
                if prev is not None:
                    pv_kt(prev, kt)
                dep_pv = ([tk_pvkt.get((prev, kt))] if prev is not None
                          else [])
                src = bass.AP(H.psb, hf, [[2048, 128], [1024, 2], [1, 512]])
                tok_pe = b.s([tok_sco] + dep_pv,
                             lambda e, kt=kt, src=src: e.activation(
                                 H.pffn[:, :, kt, :], src,
                                 AF.Exp, bias=cells[:, 2:3],
                                 scale=cells[:, 1:2]))
                bankF[sb_e] = tok_pe
                bankF[sb_o] = tok_pe
                tk_ps_all[(pp, kt)] = tok_pe
                tk_scolast[hp] = tok_sco
            if hp == NCH - 1 and g + 1 < NQG:
                if prev is not None:
                    qproj(g + 1)
                    pend[0] = norm_recips(prev)
                else:
                    qproj(g + 1)
            elif prev is not None:
                pend[0] = norm_recips(prev)
            prev = pp
            npair += 1
    for kt in range(NTB):
        pv_kt(prev, kt)
        if kt == 6 and pend[0] is not None:
            tok_attn = norm_tail(*pend[0])
            pend[0] = None
    tok_attn = norm_tail(*norm_recips(prev))
    tk_rope_prev = tk_rope_prev[0]
    _fence(b)

    wf_src = H.d_wf.ap().rearrange("(c p) j -> p c j", p=128)
    tok_wf = b.dma("q", [tok_attn], "de", lambda e: e.dma_start(H.ffnw[:], wf_src))
    b.dma("q", [tok_attn], "dd", lambda e: e.dma_start(H.ebsb[:], H.d_eb.ap()))
    tok_eb = b.dma("q", [], "dd", lambda e: e.dma_start(H.seln[:], H.d_seln.ap()))

    tk_wtg = {}
    tk_gpost = {}
    tok_g = None

    def gates_pre(g):
        gsl = slice(g * 512, (g + 1) * 512)
        for ci in range(NCH):
            deps = [tok_attn, tok_dd] if ci == 0 else []
            if ci == 0:
                deps.append(bankF[0])
            b.t(deps, lambda e, g=g, ci=ci, gsl=gsl: e.matmul(
                _bank(H, 0)[0:8, :], H.gw[:, ci * 8:(ci + 1) * 8],
                H.attn[:, ci, gsl], start=(ci == 0), stop=(ci == NCH - 1)),
                inc=None if ci < NCH - 1 else "tn")
        tok_gl = ("tn", b.cnt["tn"])
        tok_gb2 = b.v([tok_gl, tok_dy, tok_g], lambda e: e.tensor_scalar(
            H.tmpA[0:8, 0:512], _bank(H, 0)[0:8, :], H.gbcol[:], None,
            ALU.add))
        tk_t1 = {}
        for tq in range(4):
            b.t([tok_gb2], lambda e, tq=tq: e.transpose(
                _bank(H, 0)[:, tq * 8:(tq + 1) * 8],
                H.tmpA[0:8, tq * 128:(tq + 1) * 128],
                H.idf[0:8, 0:8]), inc=None if tq < 3 else "tn")
        tok_t1 = ("tn", b.cnt["tn"])
        gg_ = H.gsc
        tok_wm = None
        for tq in range(4):
            bsl = _bank(H, 0)[:, tq * 8:(tq + 1) * 8]
            tok_nm = b.v([tok_t1, tok_wm], lambda e, bsl=bsl: e.reduce_max(
                gg_[:, 8:9], bsl, AX.X, negate=True))
            tok_ex = b.s([tok_nm], lambda e, bsl=bsl: e.activation(
                gg_[:, 0:8], bsl, AF.Exp, bias=gg_[:, 8:9]))
            tok_s1 = b.v([tok_ex], lambda e: e.reduce_sum(
                gg_[:, 9:10], gg_[:, 0:8], AX.X))
            tok_r1 = b.v([tok_s1], lambda e: e.reciprocal(
                gg_[:, 10:11], gg_[:, 9:10]))
            tok_gt = b.v([tok_r1], lambda e: e.tensor_scalar(
                gg_[:, 12:20], gg_[:, 0:8], gg_[:, 10:11], None, ALU.mult))
            tok_m1 = b.v([tok_gt], lambda e: e.reduce_max(
                gg_[:, 8:9], gg_[:, 12:20], AX.X))
            tok_k1 = b.v([tok_m1], lambda e: e.tensor_scalar(
                gg_[:, 0:8], gg_[:, 12:20], gg_[:, 8:9], None, ALU.is_ge))
            tok_g2 = b.v([tok_k1], lambda e: e.scalar_tensor_tensor(
                gg_[:, 0:8], gg_[:, 0:8], -1e30, gg_[:, 12:20],
                ALU.mult, ALU.add))
            tok_m2 = b.v([tok_g2], lambda e: e.reduce_max(
                gg_[:, 9:10], gg_[:, 0:8], AX.X))
            tok_k2 = b.v([tok_m2], lambda e: e.tensor_scalar(
                gg_[:, 0:8], gg_[:, 12:20], gg_[:, 9:10], None, ALU.is_ge))
            tok_wm = b.v([tok_k2], lambda e, tq=tq: e.tensor_tensor(
                gg_[:, 24 + tq * 8:32 + tq * 8], gg_[:, 0:8],
                gg_[:, 12:20], ALU.mult))
        tk_gpost[g] = tok_wm

    def gates_post(g):
        gsl = slice(g * 512, (g + 1) * 512)
        gg_ = H.gsc
        for tq in range(4):
            b.t([tk_gpost[g]], lambda e, tq=tq: e.matmul(
                _bank(H, 0)[0:8, tq * 128:(tq + 1) * 128],
                gg_[:, 24 + tq * 8:32 + tq * 8], H.idf[:, 0:128],
                start=True, stop=True, is_transpose=True),
                inc=None if tq < 3 else "tn")
        tok_t2 = ("tn", b.cnt["tn"])
        tok_wt = b.v([tok_t2], lambda e, gsl=gsl: e.tensor_copy(
            H.wtbf[0:8, gsl], _bank(H, 0)[0:8, 0:512]))
        bankF[0] = tok_wt
        tk_wtg[g] = tok_wt

    gates_pre(0)
    gates_post(0)

    ew_slabs = [H.ewa, H.ewb, H.ewc, H.ewd]
    ew_sems = ["db", "dc", "da", "dm"]
    tk_slab = {}
    tk_slab_used = {}
    tk_iter = {}

    def ew_src(dc, q):
        return H.d_ew.ap()[dc, q].rearrange("n d p j -> p (n d) j")

    NIT = NQG * 4
    for pre in range(3):
        pdout, pq = pre % NCH, 0
        tok_sl = b.dma("q", [tok_attn], ew_sems[pre % 4],
                       lambda e, pdout=pdout: e.dma_start(
                           ew_slabs[pre % 4][:], ew_src(pdout, 0)))
        tk_slab[pre] = tok_sl

    tok_moe = None
    tk_moe = {}
    tk_x_prev = None

    tk_u = {}

    def ffnproj_mm(g, lo, hi):
        nonlocal tok_g
        gsl = slice(g * 512, (g + 1) * 512)
        for dout in range(lo, hi):
            bk = 1 + ((dout + g) % 3)
            for ci in range(NCH):
                deps = [tk_moe[(g, ci)]]
                if ci == 0:
                    deps += [tok_wf, bankF[bk]]
                b.t(deps, lambda e, dout=dout, ci=ci, gsl=gsl, bk=bk:
                    e.matmul(
                    _bank(H, bk), H.ffnw[:, ci, dout * 128:(dout + 1) * 128],
                    H.moe[:, ci, gsl], start=(ci == 0), stop=(ci == NCH - 1)),
                    inc=None if ci < NCH - 1 else "tn")
            tok_fp = ("tn", b.cnt["tn"])
            tok_u = b.v([tok_fp, tok_g], lambda e, dout=dout, g=g, bk=bk:
                        e.tensor_tensor(H.ustg[:, dout, :], _bank(H, bk),
                                        H.xnp[:, dout,
                                              1 + g * 512:1 + (g + 1) * 512],
                                        ALU.add))
            tk_u[dout] = tok_u
            bankF[bk] = tok_u

    def ffnproj_silu(g):
        nonlocal tok_g
        gsl = slice(g * 512, (g + 1) * 512)
        tok_ch7 = ("tn", b.cnt["tn"])
        for dout in range(NCH):
            tok_g = b.s([tk_u[dout], tok_ch7], lambda e, dout=dout, gsl=gsl:
                        e.activation(
                H.g_[:, dout, gsl], H.ustg[:, dout, :], AF.Silu,
                bias=H.biasP[:, 16 + dout:17 + dout]))

    for g in range(NQG):
        gsl = slice(g * 512, (g + 1) * 512)
        tk_wr = {}
        pend_fp = g - 1
        for n in range(NE):
            bk = n % 2
            dep_rp = [tk_wtg[g], bankF[bk], tok_eb]
            if g == 0 and n < 2:
                dep_rp.append(tok_attn)
            tok_rp = b.t(dep_rp, lambda e, n=n, gsl=gsl: e.matmul(
                _bank(H, bk), H.seln[0:8, n * 128:(n + 1) * 128],
                H.wtbf[0:8, gsl], start=True, stop=True))
            tok_we = b.v([tok_rp, tk_x_prev], lambda e, n=n, bk=bk: e.tensor_copy(
                H.wrep[:, n, :], _bank(H, bk)))
            bankF[bk] = tok_we
            tk_wr[n] = tok_we
        if pend_fp >= 0:
            ffnproj_mm(pend_fp, 4, NCH)
            ffnproj_silu(pend_fp)
        for h in range(2):
            for dq in range(2):
                it = (g * 2 + h) * 2 + dq
                xb_ = it % 2
                tk_x = {}
                for n in range(NE):
                    for dic in range(2):
                        ci = (2 * h + dq) * 2 + dic
                        tk_x[(n, dic)] = b.v(
                            [tk_wr[n], tk_iter.get(it - 2)],
                            lambda e, n=n, dic=dic, ci=ci, xb_=xb_, gsl=gsl:
                            e.tensor_tensor(H.xsl[:, xb_, n, dic, :],
                                            H.attn[:, ci, gsl],
                                            H.wrep[:, n, :], ALU.mult))
                tk_x_prev = tk_x[(NE - 1, 1)]
                q = 2 * h + dq
                for dout in range(NCH):
                    sidx = it * NCH + dout
                    si = sidx % 4
                    nsidx = sidx + 3
                    if nsidx < NIT * NCH:
                        nit, ndout = nsidx // NCH, nsidx % NCH
                        nq = (nit % 4)
                        nsi = nsidx % 4
                        dep_pf = [tk_slab_used.get(("mm", nsidx - 4))
                                  or tok_attn]
                        tok_pf = b.dma("q", dep_pf, ew_sems[nsi],
                                       lambda e, ndout=ndout, nq=nq, nsi=nsi:
                                       e.dma_start(ew_slabs[nsi][:],
                                                   ew_src(ndout, nq)))
                        tk_slab[nsidx] = tok_pf
                    bk = dout % 8
                    first = (h == 0 and dq == 0)
                    last_iter = (h == 1 and dq == 1)
                    for n in range(NE):
                        for dic in range(2):
                            deps = [tk_x[(n, dic)]]
                            if n == 0 and dic == 0:
                                deps += [tk_slab[sidx]]
                                if first:
                                    deps.append(bankF[bk])
                            is_last_mm = (n == NE - 1 and dic == 1)
                            b.t(deps, lambda e, si=si, n=n, dic=dic, xb_=xb_,
                                       bk=bk, st=(first and n == 0 and dic == 0):
                                e.matmul(
                                    _bank(H, bk), ew_slabs[si][:, n * 2 + dic, :],
                                    H.xsl[:, xb_, n, dic, :],
                                    start=st, stop=False),
                                inc="tn" if is_last_mm else None)
                    tok_grp = ("tn", b.cnt["tn"])
                    tk_slab_used[("mm", sidx)] = tok_grp
                    if last_iter:
                        tok_ebm = b.t([tok_eb], lambda e, dout=dout, gsl=gsl,
                                      bk=bk: e.matmul(
                            _bank(H, bk),
                            H.ebsb[:, dout * 128:(dout + 1) * 128],
                            H.wtbf[0:8, gsl], start=False, stop=True))
                        tok_moe = b.s([tok_ebm], lambda e, dout=dout, gsl=gsl,
                                      bk=bk: e.activation(
                            H.moe[:, dout, gsl], _bank(H, bk),
                            AF.Identity, bias=H.scal[:, 7:8]))
                        tk_moe[(g, dout)] = tok_moe
                        bankF[bk] = tok_moe
                tk_iter[it] = tok_grp
        if g + 1 < NQG:
            gates_pre(g + 1)
            ffnproj_mm(g, 0, 4)
            gates_post(g + 1)
        else:
            ffnproj_mm(g, 0, 4)
            ffnproj_mm(g, 4, NCH)
            ffnproj_silu(g)

    tok_mm31 = ("tn", b.cnt["tn"])
    tok_mk = b.dma("q", [tok_mm31], "dm",
                   lambda e: e.dma_start(H.mask[:], H.d_mask.ap()))

    ww_src = H.d_ww.ap().rearrange("(c p) j -> p c j", p=128)
    wr_src = H.d_wr.ap().rearrange("(c p) j -> p c j", p=128)
    tok_ww = b.dma("q", [tok_mm31, tok_g], "da",
               lambda e: e.dma_start(H.wwt[:], ww_src))
    tok_wr2 = b.dma("q", [tok_mm31], "dc",
                    lambda e: e.dma_start(H.wrt[:], wr_src))

    _fence(b)
    tok_jt = b.v([tok_lg, tok_dy, tok_g], lambda e: e.tensor_scalar(
        H.tmpA[:, 0:512], H.jtab[:], H.scal[:, 5:6], None, ALU.mult))
    tk_A = {}
    for i, dl in enumerate(DELTAS):
        tok_bd = b.v([tok_lg], lambda e, i=i, dl=dl: e.tensor_scalar(
            H.gsc[:, i:i + 1], H.scal[:, 5:6], float(dl - 1), None, ALU.mult))
        tok_a = b.s([tok_jt, tok_bd, tok_g], lambda e, i=i: e.activation(
            H.aslab[:, i, :], H.tmpA[:, 0:512], AF.Exp, bias=H.gsc[:, i:i + 1]))
        if i < 4:
            tok_a = b.v([tok_a, tok_mk], lambda e, i=i: e.tensor_tensor(
                H.aslab[:, i, :], H.aslab[:, i, :], H.mask[:, i, :], ALU.mult))
        tk_A[i] = tok_a

    tok_vl = None
    for tb in range(NTB):
        for dg in range(2):
            bk = (tb * 2 + dg) % 4
            for ci in range(NCH):
                deps = [tok_ww] if ci == 0 else []
                if ci == 0:
                    deps.append(bankF[bk])
                b.t(deps, lambda e, tb=tb, dg=dg, ci=ci, bk=bk: e.matmul(
                    _bank(H, bk), xnp_tok(ci, 1 + tb * 128, 128),
                    H.wwt[:, ci, dg * 512:(dg + 1) * 512],
                    start=(ci == 0), stop=(ci == NCH - 1)),
                    inc=None if ci < NCH - 1 else "tn")
            tok_mm = ("tn", b.cnt["tn"])
            tok_vl = b.v([tok_mm], lambda e, tb=tb, dg=dg, bk=bk: e.tensor_copy(
                H.vlin[:, tb, dg * 512:(dg + 1) * 512], _bank(H, bk)))
            bankF[bk] = tok_vl

    tk_store = {}
    nst = [0, 0]
    tk_xl = {}

    def emit_plin(g):
        tk_P = {}
        for tp in range(4 * g + 4):
            ai = 4 * g - tp + 3
            bk = tp % 4
            for ci in range(NCH):
                deps = [tok_vl] if ci == 0 else []
                if ci == 0:
                    deps.append(bankF[bk])
                b.t(deps, lambda e, tp=tp, ci=ci, g=g, bk=bk: e.matmul(
                    _bank(H, bk), xnp_tok(ci, tp * 128, 128),
                    xnp_tok(ci, 1 + g * 512, 512),
                    start=(ci == 0), stop=(ci == NCH - 1)),
                    inc=None if ci < NCH - 1 else "tn")
            tok_mm = ("tn", b.cnt["tn"])
            tk_P[tp] = b.v([tok_mm, tk_A[ai]],
                           lambda e, tp=tp, ai=ai, bk=bk: e.tensor_tensor(
                               H.plin[:, tp, :], _bank(H, bk),
                               H.aslab[:, ai, :], ALU.mult))
            bankF[bk] = tk_P[tp]
        return tk_P

    def emit_reads(g, tk_P):
        tk_rd = {}
        for ci in range(NCH):
            bk = 4 + (ci % 2)
            ntp = 4 * g + 4
            for tp in range(ntp):
                deps = [tk_P[tp]]
                if tp == 0:
                    deps.append(bankF[bk])
                b.t(deps, lambda e, ci=ci, tp=tp, bk=bk, ntp=ntp: e.matmul(
                    _bank(H, bk), H.vlin[:, tp, ci * 128:(ci + 1) * 128],
                    H.plin[:, tp, :], start=(tp == 0), stop=(tp == ntp - 1)),
                    inc=None if tp < ntp - 1 else "tn")
            tok_mm = ("tn", b.cnt["tn"])
            tk_rd[ci] = b.v([tok_mm], lambda e, ci=ci, bk=bk: e.tensor_copy(
                H.reads[:, ci, :], _bank(H, bk)))
            bankF[bk] = tk_rd[ci]
        return tk_rd

    tk_P = emit_plin(0)
    tk_rd = emit_reads(0, tk_P)
    for g in range(NQG):
        tk_sg = {}
        for co in range(NCH):
            bk = 6 + (co % 2)
            for ci in range(NCH):
                deps = [tok_wr2, tk_rd[ci]]
                if ci == 0:
                    deps.append(bankF[bk])
                b.t(deps, lambda e, co=co, ci=ci, bk=bk: e.matmul(
                    _bank(H, bk), H.wrt[:, ci, co * 128:(co + 1) * 128],
                    H.reads[:, ci, :], start=(ci == 0), stop=(ci == NCH - 1)),
                    inc=None if ci < NCH - 1 else "tn")
            tok_mm = ("tn", b.cnt["tn"])
            tok_sf = b.v([tok_mm, tok_alv, tk_A[15], tk_sg.get(co - 1)],
                         lambda e, co=co, g=g, bk=bk:
                         e.scalar_tensor_tensor(
                             H.tmpA[:, 0:512], _bank(H, bk), H.scal[:, 6:7],
                             H.xnp[:, co, 1 + g * 512:1 + (g + 1) * 512],
                             ALU.mult, ALU.add))
            bankF[bk] = tok_sf
            tk_sg[co] = b.v([tok_sf, tok_g], lambda e, co=co, g=g:
                            e.tensor_tensor(H.sgf[:, co, :], H.tmpA[:, 0:512],
                                            H.g_[:, co, g * 512:(g + 1) * 512],
                                            ALU.mult))
        if g + 1 < NQG:
            tk_P = emit_plin(g + 1)
        for tq in range(4):
            tb = g * 4 + tq
            dep_xl = [tk_store.get(tb - 4)]
            if tb < 4:
                dep_xl.append(tok_mm31)
            tk_xl[tb] = b.dma("q", dep_xl, "dl%d" % (tb % 4),
                              lambda e, tb=tb: e.dma_start(
                                  H.xtml[:, tb % 4, :],
                                  H.d_x.ap()[tb * 128:(tb + 1) * 128, :]))
        for tq in range(4):
            tb = g * 4 + tq
            par = tb % 2
            baseA = 4 if par == 0 else 6
            tok_xl = tk_xl[tb]
            tok_add = None
            for half in range(2):
                bk = baseA + half
                for c4 in range(4):
                    co = half * 4 + c4
                    deps = [tk_sg[co]]
                    if c4 == 0:
                        deps.append(bankF[bk])
                    b.t(deps, lambda e, co=co, tq=tq, bk=bk, c4=c4: e.matmul(
                        _bank(H, bk)[:, c4 * 128:(c4 + 1) * 128],
                        H.sgf[:, co, tq * 128:(tq + 1) * 128], H.idf[:],
                        start=True, stop=True, is_transpose=True),
                        inc=None if c4 < 3 else "tn")
                tok_tp = ("tn", b.cnt["tn"])
                dep_st = [tok_tp, tok_xl]
                if tb >= 2:
                    dep_st.append(tk_store.get(tb - 2))
                tok_add = b.v(dep_st, lambda e, half=half, par=par, bk=bk,
                              tb=tb: e.tensor_tensor(
                                  H.outtm[:, par, half * 512:(half + 1) * 512],
                                  _bank(H, bk), H.xtml[:, tb % 4,
                                  half * 512:(half + 1) * 512], ALU.add))
                bankF[bk] = tok_add
            tok_st = b.dma("y", [tok_add], "ds0" if par == 0 else "ds1",
                           lambda e, tb=tb, par=par: e.dma_start(
                               H.d_out.ap()[tb * 128:(tb + 1) * 128, :],
                               H.outtm[:, par, :]))
            nst[par] += 1
            tk_store[tb] = tok_st
        if g + 1 < NQG:
            tk_rd = emit_reads(g + 1, tk_P)

    b.wait("y", [("ds0", 16 * nst[0]), ("ds1", 16 * nst[1])])


def _host_prep(inputs):
    A = np.ascontiguousarray
    sh = {}
    sh["wqT"] = A(inputs["L0_ffn_q_w"].T)
    sh["wkT"] = A(inputs["L0_ffn_k_w"].T)
    sh["wvT"] = A(inputs["L0_ffn_v_w"].T)
    sh["wwT"] = A(inputs["L0_attn_W_write"].T)
    sh["wrT"] = A(inputs["L0_attn_W_read"].T)
    sh["wfT"] = A(inputs["L0_ffn_ffn_w"].T)
    ew = np.asarray(inputs["L0_ffn_expert_w"])
    sh["ewT"] = A(ew.reshape(NE, NE, 128, 4, 2, 128)
                  .transpose(1, 3, 0, 4, 5, 2))
    sh["eb"] = A(np.asarray(inputs["L0_ffn_expert_b"]))
    bp = np.zeros((128, 24), np.float32)
    bp[:, 0:8] = np.asarray(inputs["L0_ffn_q_b"]).reshape(8, 128).T
    bp[:, 8:16] = np.asarray(inputs["L0_ffn_k_b"]).reshape(8, 128).T
    bp[:, 16:24] = np.asarray(inputs["L0_ffn_ffn_b"]).reshape(8, 128).T
    sh["biasP"] = bp
    sh["gbcol"] = A(np.asarray(inputs["L0_ffn_gate_b"]).reshape(8, 1))
    sh["vrow"] = A(np.asarray(inputs["L0_ffn_v_b"]).reshape(1, DM))
    gw = np.asarray(inputs["L0_ffn_gate_w"])
    sh["gwT"] = A(gw.T.reshape(NE, 128, NE).transpose(1, 0, 2).reshape(128, 64))
    inv = (10000.0 ** (-np.arange(0, 64, 2, dtype=np.float64) / 64.0))
    ang = np.outer(np.arange(T, dtype=np.float64), inv)
    cos64 = np.cos(np.concatenate([ang, ang], axis=1)).T
    sin64 = np.sin(np.concatenate([ang, ang], axis=1)).T
    sh["cosT"] = np.vstack([cos64, cos64]).astype(np.float32)
    sh["sinT"] = np.vstack([sin64, sin64]).astype(np.float32)
    P = np.zeros((128, 128), np.float32)
    for blk in range(2):
        o = blk * 64
        for j in range(32):
            P[o + j + 32, o + j] = -1.0
            P[o + j, o + j + 32] = 1.0
    sh["prot"] = P
    sh["idf"] = np.eye(128, dtype=np.float32)
    sh["onesr"] = np.ones((64, 128), np.float32)
    sh["onesc"] = np.ones((128, 1), np.float32)
    sh["onesrb"] = np.ones((64, 128), np.float32)
    seln = np.zeros((NE, DM), np.float32)
    for n in range(NE):
        seln[n, n * 128:(n + 1) * 128] = 1.0
    sh["seln"] = seln
    jj = (np.arange(512)[None, :] - np.arange(128)[:, None]).astype(np.float32)
    sh["jtab"] = A(jj)
    masks = np.zeros((128, 4, 512), np.float32)
    for i in range(4):
        dl = DELTAS[i]
        masks[:, i, :] = ((np.arange(512)[None, :] - np.arange(128)[:, None]
                           + dl) > 0).astype(np.float32)
    sh["maskT"] = masks
    sh["decay"] = np.asarray(inputs["L0_attn_decay"],
                             np.float32).reshape(1, 1)
    sh["log_alpha"] = np.asarray(inputs["L0_attn_log_alpha"],
                                 np.float32).reshape(1, 1)
    return sh


def kernel(**inputs):
    sh = _host_prep(inputs)
    x = np.asarray(inputs["L0_attn_out"], np.float32)
    nc, H = _build_nc()
    in_maps = []
    for core in range(8):
        m = dict(sh)
        m["x"] = np.ascontiguousarray(x[core])
        in_maps.append(m)
    res = run_bass_kernel_spmd(nc, in_maps, core_ids=list(range(8)))
    out = np.stack([r["out"] for r in res.results], axis=0)
    return out


# version 46 (final)
# speedup vs baseline: 1.0189x; 1.0189x over previous
import math

import numpy as np
import ml_dtypes
import concourse.bass as bass
import concourse.mybir as mybir
from concourse import dve_ops
from concourse.bass_utils import run_bass_kernel_spmd

F32 = mybir.dt.float32
BF16 = mybir.dt.bfloat16
AF = mybir.ActivationFunctionType
ALU = mybir.AluOpType
AX = mybir.AxisListType

T = 2048
DM = 1024
NCH = 8
CW = 2052
NTB = 16
NQG = 4
NE = 8

PERS = {}
_pc = [0]
def _pers(name, nbytes):
    off = _pc[0]
    PERS[name] = off
    _pc[0] = (off + nbytes + 63) & ~63
    return off

O_XNP   = _pers("xnp", NCH * CW * 2)
O_BIASP = _pers("biasP", 24 * 4)
O_PROT  = _pers("prot", 128 * 2)
O_IDF   = _pers("idf", 128 * 4)
O_ONESR = _pers("onesr", 128 * 4)
O_SCAL  = _pers("scal", 16 * 4)
O_GSC   = _pers("gsc", 64 * 4)
O_ONESC = _pers("onesc", 64)
O_ONESRB= _pers("onesrb", 128 * 2)
O_GBCOL = _pers("gbcol", 64)
O_TMPA  = _pers("tmpA", 512 * 4)
O_J     = _pers("jtab", 512 * 4)
O_GW    = _pers("gw", 64 * 2)
ZB = (_pc[0] + 63) & ~63

Z = lambda r: ZB + r
O_XFM  = Z(0)
O_K    = Z(0)
O_MOE  = Z(0)
O_VLIN = Z(106496)
O_VFFN = Z(32768)
O_FFNW = Z(32768)
O_EWA  = Z(49152)
O_EWB  = Z(53248)
O_SELN = Z(57344)
O_EBSB = Z(59392)
O_WTBF = Z(61440)
O_WREP = Z(65536)
O_ASLAB= Z(32768)
O_OUTTM= Z(49152)
O_XTML = Z(57344)
O_ATTN = Z(73728)
O_STM  = Z(73728)
O_STR  = Z(81920)
O_XSQ  = Z(90112)
O_PTMP = Z(92160)
O_VROW = Z(73728)
O_PLIN = Z(73728)
O_MASK = Z(90112)
O_READS= Z(94208)
O_PFFN = Z(106496)
O_X    = Z(106496)
O_G    = Z(0)
O_WVT  = Z(90112)
O_EWC  = Z(139264)
O_EWD  = Z(143360)
O_XTMP = Z(122880)
O_XBG  = Z(131072)
O_COS  = Z(139264)
O_SIN  = Z(143360)
O_Q0   = Z(147456)
O_WRT  = Z(139264)
O_WKT  = Z(155648)
O_WQT  = Z(155648)
O_WWT  = Z(73728)
O_SGF  = Z(155648)
O_USTG = Z(155648)
O_FSC  = Z(172032)
ZEND = Z(173056)

SEMS = ["tn", "vc", "sc", "da", "db", "dc", "dd", "dy", "de", "dm",
        "dx0", "dx1", "ds0", "ds1", "dl0", "dl1", "dl2", "dl3", "yc", "dw"]

DELTAS = [-384 + 128 * i for i in range(16)]


class _B:

    def __init__(self, nc, who, eng, sems):
        self.nc = nc
        self.who = who
        self.eng = eng
        self.sems = sems
        self.cnt = {k: 0 for k in sems}
        self.hw = {k: 0 for k in sems}

    def _waits(self, deps):
        for d in deps:
            if d is None:
                continue
            s, v = d
            if v > self.hw[s]:
                self.eng.wait_ge(self.sems[s], v)
                self.hw[s] = v

    def op(self, owner, deps, inc, fn, amt=1):
        if self.who == owner:
            self._waits(deps)
            ins = fn(self.eng)
            if inc is not None:
                ins.then_inc(self.sems[inc], amt)
        if inc is not None:
            self.cnt[inc] += amt
            return (inc, self.cnt[inc])
        return None

    def t(self, deps, fn, inc="tn"):
        return self.op("t", deps, inc, fn)

    def v(self, deps, fn, inc="vc"):
        return self.op("v", deps, inc, fn)

    def s(self, deps, fn, inc="sc"):
        return self.op("s", deps, inc, fn)

    def y(self, deps, fn, inc="yc"):
        return self.op("q", deps, inc, fn)

    def dma(self, owner, deps, sem, fn):
        return self.op(owner, deps, sem, fn, amt=16)

    def wait(self, owner, deps):
        if self.who == owner:
            self._waits(deps)


class _H:
    pass


def _build_nc():
    nc = bass.Bass("TRN2", target_bir_lowering=False)
    H = _H()
    H.nc = nc

    def din(name, shape, dt=F32):
        return nc.dram_tensor(name, list(shape), dt, kind="ExternalInput")

    H.d_x = din("x", [T, DM])
    H.d_wq = din("wqT", [DM, DM])
    H.d_wk = din("wkT", [DM, DM])
    H.d_wv = din("wvT", [DM, DM])
    H.d_ww = din("wwT", [DM, DM])
    H.d_wr = din("wrT", [DM, DM])
    H.d_wf = din("wfT", [DM, DM])
    H.d_ew = din("ewT", [NE, 4, NE, 2, 128, 128])
    H.d_eb = din("eb", [NE, DM])
    H.d_biasP = din("biasP", [128, 24])
    H.d_gb = din("gbcol", [NE, 1])
    H.d_vrow = din("vrow", [1, DM])
    H.d_gw = din("gwT", [128, 64])
    H.d_cos = din("cosT", [128, T])
    H.d_sin = din("sinT", [128, T])
    H.d_prot = din("prot", [128, 128])
    H.d_idf = din("idf", [128, 128])
    H.d_onesr = din("onesr", [64, 128])
    H.d_onesc = din("onesc", [128, 1])
    H.d_onesrb = din("onesrb", [64, 128])
    H.d_j = din("jtab", [128, 512])
    H.d_mask = din("maskT", [128, 4, 512])
    H.d_seln = din("seln", [NE, DM])
    H.d_dec = din("decay", [1, 1])
    H.d_la = din("log_alpha", [1, 1])
    H.d_out = nc.dram_tensor("out", [T, DM], F32, kind="ExternalOutput")
    H.d_ewb = nc.dram_tensor("ewbf", [NE, 4, NE, 2, 128, 128], BF16,
                             kind="Internal")
    H.d_wfb = nc.dram_tensor("wfbf", [DM, DM], BF16, kind="Internal")
    H.d_wwb = nc.dram_tensor("wwbf", [DM, DM], BF16, kind="Internal")
    H.d_wrb = nc.dram_tensor("wrbf", [DM, DM], BF16, kind="Internal")

    H.arena = nc.alloc_sbuf_tensor("arena", [128, ZEND], mybir.dt.uint8)
    A0 = nc.lookup_mloc(H.arena).addr

    def sb(name, shape, dt, off):
        return nc.alloc_sbuf_tensor_at(name, list(shape), dt, offset=A0 + off)

    H.xnp = sb("xnp", [128, NCH, CW], BF16, O_XNP)
    H.biasP = sb("biasP_s", [128, 24], F32, O_BIASP)
    H.prot = sb("prot_s", [128, 128], BF16, O_PROT)
    H.idf = sb("idf_s", [128, 128], F32, O_IDF)
    H.onesr = sb("onesr_s", [64, 128], F32, O_ONESR)
    H.scal = sb("scal", [128, 16], F32, O_SCAL)
    H.gsc = sb("gsc", [128, 64], F32, O_GSC)
    H.onesc = sb("onesc_s", [128, 1], BF16, O_ONESC)
    H.onesrb = sb("onesrb_s", [64, 128], BF16, O_ONESRB)
    H.gbcol = sb("gbcol_s", [NE, 1], F32, O_GBCOL)
    H.tmpA = sb("tmpA", [128, 512], F32, O_TMPA)
    H.jtab = sb("jtab_s", [128, 512], F32, O_J)
    H.gw = sb("gw_s", [128, 64], BF16, O_GW)
    H.seln = sb("seln_s", [NE, DM], BF16, O_SELN)
    H.vrow = sb("vrow_s", [1, DM], BF16, O_VROW)

    H.xfm = sb("xfm", [128, NCH, T], F32, O_XFM)
    H.k_ = sb("k_", [128, NCH, T], BF16, O_K)
    H.moe = sb("moe", [128, NCH, T], BF16, O_MOE)
    H.vlin = sb("vlin", [128, NTB, DM], BF16, O_VLIN)
    H.vffn = sb("vffn", [128, NTB, NE, 160], BF16, O_VFFN)
    H.ffnw = sb("ffnw", [128, NCH, DM], BF16, O_FFNW)
    H.ewa = sb("ewa", [128, 16, 128], BF16, O_EWA)
    H.aslab = sb("aslab", [128, 16, 512], BF16, O_ASLAB)
    H.outtm = sb("outtm", [128, 2, DM], F32, O_OUTTM)
    H.xtml = sb("xtml", [128, 4, DM], F32, O_XTML)
    H.attn = sb("attn", [128, NCH, T], BF16, O_ATTN)
    H.stm = sb("stm", [1, T], F32, O_STM)
    H.str_ = sb("str", [1, T], F32, O_STR)
    H.xsq = sb("xsq", [128, 2, 512], BF16, O_XSQ)
    H.ptmp = sb("ptmp", [128, 512], F32, O_PTMP)
    H.plin = sb("plin", [128, 16, 512], BF16, O_PLIN)
    H.mask = sb("mask", [128, 4, 512], BF16, O_MASK)
    H.reads = sb("reads", [128, NCH, 512], BF16, O_READS)
    H.wvt = sb("wvt", [128, NCH, DM], BF16, O_WVT)
    H.xtmp = sb("xtmp", [128, 2, DM], F32, O_XTMP)
    H.xbg = sb("xbg", [128, NCH, 512], BF16, O_XBG)
    H.pffn = sb("pffn", [128, 2, 16, 512], BF16, O_PFFN)
    H.xtmp4 = sb("xtmp4", [128, 4, DM], F32, O_PFFN)
    H.xsl = sb("xsl", [128, 2, NE, 2, 512], BF16, O_X)
    H.g_ = sb("g_", [128, NCH, T], BF16, O_G)
    H.ustg = sb("ustg", [128, NCH, 512], F32, O_USTG)
    H.cos = sb("cos", [128, T], BF16, O_COS)
    H.sin = sb("sin", [128, T], BF16, O_SIN)
    H.q0 = sb("q0", [128, NCH, 512], BF16, O_Q0)
    H.k0 = sb("k0", [128, 2, T], BF16, O_Q0)
    H.ewb = sb("ewb", [128, 16, 128], BF16, O_EWB)
    H.ewc = sb("ewc", [128, 16, 128], BF16, O_EWC)
    H.ewd = sb("ewd", [128, 16, 128], BF16, O_EWD)
    H.wrt = sb("wrt", [128, NCH, DM], BF16, O_WRT)
    H.wkt = sb("wkt", [128, NCH, DM], BF16, O_WKT)
    H.wqt = sb("wqt", [128, NCH, DM], BF16, O_WQT)
    H.wtbf = sb("wtbf", [NE, T], BF16, O_WTBF)
    H.ebsb = sb("ebsb", [NE, DM], BF16, O_EBSB)
    H.wrep = sb("wrep", [128, NE, 512], BF16, O_WREP)
    H.wwt = sb("wwt", [128, NCH, DM], BF16, O_WWT)
    H.sgf = sb("sgf", [128, NCH, 512], F32, O_SGF)
    H.fsc = sb("fsc", [128, 512], BF16, O_FSC)

    H.psb = nc.alloc_psum_tensor("psb", [128, 2048], F32)
    H.ps = [None, None,
            nc.alloc_psum_tensor("ps2", [128, 1024], F32),
            nc.alloc_psum_tensor("ps3", [128, 1024], F32)]

    H.sems = {name: nc.alloc_semaphore(name) for name in SEMS}

    with nc.Block() as block:

        @block.sync
        def _(e):
            _program(_B(nc, "y", e, H.sems), H)

        @block.gpsimd
        def _(e):
            _program(_B(nc, "q", e, H.sems), H)

        @block.tensor
        def _(e):
            _program(_B(nc, "t", e, H.sems), H)

        @block.vector
        def _(e):
            _program(_B(nc, "v", e, H.sems), H)

        @block.scalar
        def _(e):
            _program(_B(nc, "s", e, H.sems), H)

    return nc, H


def _bank(H, b):
    if b < 4:
        return H.psb[:, b * 512:(b + 1) * 512]
    return H.ps[b // 2][:, (b % 2) * 512:(b % 2 + 1) * 512]


def _fence(b: _B):
    toks = [(sname, b.cnt[sname]) for sname in ("tn", "vc", "sc")]
    for who in ("t", "v", "s", "q", "y"):
        b.wait(who, toks)


def _program(b: _B, H):
    nc = H.nc
    bankF = [None] * 8
    cells = H.scal

    tok_idf = None
    for dst, src in [(H.idf[:], H.d_idf.ap()), (H.biasP[:], H.d_biasP.ap()),
                     (H.onesr[:], H.d_onesr.ap()), (H.gbcol[:], H.d_gb.ap()),
                     (H.jtab[:], H.d_j.ap()),
                     (cells[0:1, 14:15], H.d_dec.ap()),
                     (cells[0:1, 15:16], H.d_la.ap())]:
        tok_dy = b.dma("y", [], "dy", lambda e, d=dst, s=src: e.dma_start(d, s))
        if tok_idf is None:
            tok_idf = tok_dy
    for dst, src in [(H.cos[:], H.d_cos.ap()), (H.sin[:], H.d_sin.ap()),
                     (H.prot[:], H.d_prot.ap()), (H.onesc[:], H.d_onesc.ap()),
                     (H.onesrb[:], H.d_onesrb.ap()), (H.gw[:], H.d_gw.ap())]:
        tok_dd = b.dma("q", [], "dd", lambda e, d=dst, s=src: e.dma_start(d, s))

    b.v([], lambda e: e.memset(cells[:, 0:1], -1.0))
    b.v([], lambda e: e.memset(cells[:, 1:2], 0.125))
    b.v([], lambda e: e.memset(cells[:, 2:3], -8.0))
    b.v([], lambda e: e.memset(cells[:, 3:4], 1e-5))
    b.v([], lambda e: e.memset(cells[:, 4:5], -0.5))
    b.v([], lambda e: e.memset(cells[:, 7:8], 0.0))
    tok_cells = b.v([], lambda e: e.memset(H.xnp[:, :, 0:1], 0.0))

    c = lambda i: cells[0:1, i:i + 1]
    tok_z = b.s([tok_dy, tok_cells],
                lambda e: e.activation(c(8), c(14), AF.Exp, scale=cells[0:1, 0:1]))
    tok_acc = b.v([tok_z], lambda e: e.memset(c(9), -1.0 / 12.0))
    for k in range(11, 0, -1):
        ck = ((-1.0) ** (k + 1)) / k
        tok_acc = b.v([tok_acc],
                      lambda e, ck=ck: e.tensor_scalar(c(9), c(9), c(8), ck,
                                                       ALU.mult, ALU.add))
    tok_ls = b.v([tok_acc], lambda e: e.tensor_scalar(c(10), c(9), c(8), None,
                                                      ALU.mult))
    tok_u = b.v([tok_z], lambda e: e.tensor_scalar(c(11), c(8), 1.0, None,
                                                   ALU.add))
    tok_ln = b.s([tok_u], lambda e: e.activation(c(11), c(11), AF.Ln))
    tok_w = b.v([tok_dy], lambda e: e.tensor_scalar(c(12), c(14), -2.0, 1e9,
                                                    ALU.add, ALU.mult))
    tok_w = b.v([tok_w], lambda e: e.tensor_scalar(c(12), c(12), 0.0, 1.0,
                                                   ALU.max, ALU.min))
    tok_d = b.v([tok_ls, tok_ln], lambda e: e.tensor_tensor(c(13), c(10), c(11),
                                                            ALU.subtract))
    tok_L = b.v([tok_d, tok_w],
                lambda e: e.scalar_tensor_tensor(c(10), c(13), c(12), c(11),
                                                 ALU.mult, ALU.add))
    tok_L = b.v([tok_L], lambda e: e.tensor_scalar(c(10), c(10), -1.0, None,
                                                   ALU.mult))
    tok_al = b.s([tok_dy, tok_L], lambda e: e.activation(c(11), c(15), AF.Exp))
    tok_r1 = b.t([tok_L], lambda e: e.matmul(_bank(H, 6)[:, 0:1], H.onesr[0:1, :],
                                             c(10), start=True, stop=True))
    tok_r2 = b.t([tok_al], lambda e: e.matmul(_bank(H, 7)[:, 0:1], H.onesr[0:1, :],
                                              c(11), start=True, stop=True))
    tok_lg = b.v([tok_r1], lambda e: e.tensor_copy(H.scal[:, 5:6],
                                                   _bank(H, 6)[:, 0:1]))
    tok_alv = b.v([tok_r2], lambda e: e.tensor_copy(H.scal[:, 6:7],
                                                    _bank(H, 7)[:, 0:1]))
    bankF[6] = tok_lg
    bankF[7] = tok_alv

    tk_xT = {}
    tk_xfm = {}
    tk_prev_stats = None
    dxs = ["dx0", "dx1", "dl0", "dl1"]
    for tb in range(NTB):
        g = tb // 4
        par = tb % 4
        deps = [tk_xT.get(tb - 4)]
        tok_x = b.dma("y" if par % 2 == 0 else "q", deps, dxs[par],
                      lambda e, tb=tb, par=par: e.dma_start(
                          H.xtmp4[:, par, :], H.d_x.ap()[tb * 128:(tb + 1) * 128, :]))
        for ci in range(NCH):
            bk = 4 + (ci % 4)
            tok_tp = b.t([tok_x, bankF[bk], tok_idf],
                         lambda e, tb=tb, par=par, ci=ci, bk=bk: e.transpose(
                             _bank(H, bk)[:, 0:128],
                             H.xtmp4[:, par, ci * 128:(ci + 1) * 128], H.idf[:]))
            if ci == NCH - 1:
                tk_xT[tb] = tok_tp
            tok_e = b.s([tok_tp],
                        lambda e, tb=tb, ci=ci, bk=bk: e.copy(
                            H.xfm[:, ci, tb * 128:(tb + 1) * 128],
                            _bank(H, bk)[:, 0:128]))
            bankF[bk] = tok_e
            tk_xfm[(tb, ci)] = tok_e

        if tb % 4 == 3:
            tk_xb = {}
            tk_sq = {}
            tok_sum = None
            tok_ssq = None
            for ci in range(NCH):
                dep_f = [tk_xfm[(g * 4 + j, ci)] for j in range(4)]
                dep_f.append(tk_prev_stats)
                tk_xb[ci] = b.v(dep_f, lambda e, ci=ci, g=g: e.tensor_copy(
                    H.xbg[:, ci, :], H.xfm[:, ci, g * 512:(g + 1) * 512]))
                tk_sq[ci] = b.v([tk_xb[ci], tk_sq.get(ci - 2)],
                                lambda e, ci=ci: e.tensor_tensor(
                                    H.xsq[:, ci % 2, :], H.xbg[:, ci, :],
                                    H.xbg[:, ci, :], ALU.mult))
                tok_sum = b.t([tk_xb[ci]] + ([bankF[0], tok_dd] if ci == 0 else []),
                              lambda e, ci=ci: e.matmul(
                                  _bank(H, 0)[0:1, :], H.onesc[:],
                                  H.xbg[:, ci, :], start=(ci == 0),
                                  stop=(ci == NCH - 1)))
                tok_ssq = b.t([tk_sq[ci]] + ([bankF[1]] if ci == 0 else []),
                              lambda e, ci=ci: e.matmul(
                                  _bank(H, 1)[0:1, :], H.onesc[:],
                                  H.xsq[:, ci % 2, :], start=(ci == 0),
                                  stop=(ci == NCH - 1)))
                tk_sq[ci] = tok_ssq
            gs = slice(g * 512, (g + 1) * 512)
            tok_m = b.v([tok_sum], lambda e, gs=gs: e.tensor_scalar(
                H.stm[0:1, gs], _bank(H, 0)[0:1, :], 1.0 / DM, None, ALU.mult))
            bankF[0] = tok_m
            tok_e2 = b.v([tok_ssq], lambda e: e.tensor_scalar(
                H.ptmp[0:1, :], _bank(H, 1)[0:1, :], 1.0 / DM, None, ALU.mult))
            bankF[1] = tok_e2
            tok_m2 = b.v([tok_m], lambda e, gs=gs: e.tensor_tensor(
                H.tmpA[0:1, 0:512], H.stm[0:1, gs], H.stm[0:1, gs], ALU.mult))
            tok_var = b.v([tok_e2, tok_m2], lambda e: e.tensor_tensor(
                H.ptmp[0:1, :], H.ptmp[0:1, :], H.tmpA[0:1, 0:512],
                ALU.subtract))
            tok_ln2 = b.s([tok_var], lambda e: e.activation(
                H.ptmp[0:1, :], H.ptmp[0:1, :], AF.Ln, bias=cells[0:1, 3:4]))
            tok_r = b.s([tok_ln2], lambda e, gs=gs: e.activation(
                H.str_[0:1, gs], H.ptmp[0:1, :], AF.Exp, scale=cells[0:1, 4:5]))
            tok_mr = b.t([tok_m, bankF[2]], lambda e, gs=gs: e.matmul(
                _bank(H, 2), H.onesr[0:1, :], H.stm[0:1, gs], start=True, stop=True))
            tok_rr = b.t([tok_r, bankF[3]], lambda e, gs=gs: e.matmul(
                _bank(H, 3), H.onesr[0:1, :], H.str_[0:1, gs], start=True, stop=True))
            tok_u2 = tok_r
            for ci in range(NCH):
                tok_u2 = b.v([tok_mr, tok_u2], lambda e, ci=ci, gs=gs:
                             e.tensor_tensor(H.ptmp[:], H.xfm[:, ci, gs],
                                             _bank(H, 2), ALU.subtract))
                tok_u2 = b.v([tok_rr, tok_u2], lambda e, ci=ci, g=g:
                             e.tensor_tensor(
                                 H.xnp[:, ci, 1 + g * 512:1 + (g + 1) * 512],
                                 H.ptmp[:], _bank(H, 3), ALU.mult))
            bankF[2] = tok_u2
            bankF[3] = tok_u2
            tk_prev_stats = tok_ssq
    tok_xnp = tok_u2
    wk_src = H.d_wk.ap().rearrange("(c p) j -> p c j", p=128)
    tok_wk = b.dma("q", [], "da", lambda e: e.dma_start(H.wkt[:], wk_src))
    _fence(b)

    tok_lastrep = ("tn", b.cnt["tn"])
    tok_vrow = b.dma("q", [tok_lastrep], "dd",
                     lambda e: e.dma_start(H.vrow[:], H.d_vrow.ap()))
    wv_src = H.d_wv.ap().rearrange("(c p) j -> p c j", p=128)
    tok_wv = b.dma("q", [tok_lastrep], "db",
                   lambda e: e.dma_start(H.wvt[:], wv_src))
    tok_vones = b.v([tok_xnp], lambda e: e.memset(H.vffn[:, :, :, 64:65], 1.0))
    tok_vones = b.v([tok_vones],
                    lambda e: e.memset(H.vffn[:, :, :, 65:96], 0.0))

    def xnp_tok(ci, lo, n):
        return H.xnp[:, ci, lo:lo + n]

    tk_k0 = {}
    tk_rope_prev = None
    for co in range(NCH):
        for gg in range(NQG):
            bk = (co * NQG + gg) % 4
            for ci in range(NCH):
                deps = [tok_wk, tok_xnp] if ci == 0 else []
                if ci == 0:
                    deps.append(bankF[bk])
                b.t(deps, lambda e, co=co, gg=gg, ci=ci, bk=bk: e.matmul(
                    _bank(H, bk), H.wkt[:, ci, co * 128:(co + 1) * 128],
                    xnp_tok(ci, 1 + gg * 512, 512),
                    start=(ci == 0), stop=(ci == NCH - 1)),
                    inc=None if ci < NCH - 1 else "tn")
            tok_mm = ("tn", b.cnt["tn"])
            tok_ev = b.s([tok_mm, tk_k0.get((co - 2, NQG - 1))],
                         lambda e, co=co, gg=gg, bk=bk: e.activation(
                H.k0[:, co % 2, gg * 512:(gg + 1) * 512], _bank(H, bk),
                AF.Identity, bias=H.biasP[:, 8 + co:9 + co]))
            bankF[bk] = tok_ev
            tk_k0[(co, gg)] = tok_ev
        for gg in range(NQG):
            bk = 4 + gg
            tok_rot = b.t([tk_k0[(co, gg)], bankF[bk], tok_dd],
                          lambda e, co=co, gg=gg, bk=bk: e.matmul(
                              _bank(H, bk), H.prot[:],
                              H.k0[:, co % 2, gg * 512:(gg + 1) * 512],
                              start=True, stop=True))
            gsl = slice(gg * 512, (gg + 1) * 512)
            tok_ta = b.v([tok_rot, tok_dd, tk_rope_prev],
                         lambda e, gsl=gsl, bk=bk: e.tensor_tensor(
                H.tmpA[:, 0:512], _bank(H, bk), H.sin[:, gsl], ALU.mult))
            bankF[bk] = tok_ta
            tok_kc = b.v([tok_ta], lambda e, co=co, gg=gg, gsl=gsl:
                         e.tensor_tensor(H.k_[:, co, gsl],
                                         H.k0[:, co % 2, gsl],
                                         H.cos[:, gsl], ALU.mult))
            tok_kc = b.v([tok_kc], lambda e, co=co, gsl=gsl: e.tensor_tensor(
                H.k_[:, co, gsl], H.tmpA[:, 0:512], H.k_[:, co, gsl], ALU.add))
            tk_k0[(co, gg)] = tok_kc
            tk_rope_prev = tok_kc

    tok_klast = tok_kc
    wq_src = H.d_wq.ap().rearrange("(c p) j -> p c j", p=128)
    tok_wq = b.dma("q", [tok_klast], "da",
                   lambda e: e.dma_start(H.wqt[:], wq_src))

    tok_castall = None
    for dc in range(NE):
        tok_castall = b.dma("q", [], "dw", lambda e, dc=dc: e.dma_start(
            H.d_ewb.ap()[dc].rearrange("q n d p j -> (q n d p) j"),
            H.d_ew.ap()[dc].rearrange("q n d p j -> (q n d p) j")))
    for dst, srct in [(H.d_wfb, H.d_wf), (H.d_wwb, H.d_ww),
                      (H.d_wrb, H.d_wr)]:
        tok_castall = b.dma("q", [], "dw",
                            lambda e, d=dst, s=srct: e.dma_start(d.ap(), s.ap()))

    _fence(b)

    tk_q = {}
    tk_pvkt = {}
    tk_ps_all = {}
    prev = None
    tok_attn = None

    def qproj(g):
        gsl = slice(g * 512, (g + 1) * 512)
        for co in range(NCH):
            bk = co % 4
            for ci in range(NCH):
                deps = [tok_wq, tk_scolast.get(co)] if ci == 0 else []
                if ci == 0:
                    deps.append(bankF[bk])
                b.t(deps, lambda e, co=co, ci=ci, bk=bk, g=g: e.matmul(
                    _bank(H, bk), H.wqt[:, ci, co * 128:(co + 1) * 128],
                    xnp_tok(ci, 1 + g * 512, 512),
                    start=(ci == 0), stop=(ci == NCH - 1)),
                    inc=None if ci < NCH - 1 else "tn")
            tok_mm = ("tn", b.cnt["tn"])
            tok_ev = b.s([tok_mm], lambda e, co=co, bk=bk: e.activation(
                H.q0[:, co, :], _bank(H, bk), AF.Identity,
                bias=H.biasP[:, co:co + 1]))
            bankF[bk] = tok_ev
            bkr = 1 + 2 * (co % 2)
            tok_rot = b.t([tok_ev, bankF[bkr]],
                          lambda e, co=co, bkr=bkr: e.matmul(
                              _bank(H, bkr), H.prot[:], H.q0[:, co, :],
                              start=True, stop=True))
            tok_ta = b.v([tok_rot, tk_rope_prev[0]],
                         lambda e, bkr=bkr, gsl=gsl: e.tensor_tensor(
                H.tmpA[:, 0:512], _bank(H, bkr), H.sin[:, gsl], ALU.mult))
            bankF[bkr] = tok_ta
            tok_qc = b.v([tok_ta], lambda e, co=co, gsl=gsl: e.tensor_tensor(
                H.q0[:, co, :], H.q0[:, co, :], H.cos[:, gsl], ALU.mult))
            tok_qc = b.v([tok_qc], lambda e, co=co: e.tensor_tensor(
                H.q0[:, co, :], H.tmpA[:, 0:512], H.q0[:, co, :], ALU.add))
            tk_q[co] = tok_qc
            tk_rope_prev[0] = tok_qc

    def pv_kt(pp, kt, deps_extra=()):
        pg, php = pp
        pvt = H.ps[2 + pairno[pp] % 2]
        pvb = 4 + 2 * (pairno[pp] % 2)
        for h2 in range(2):
            deps = [tk_ps_all[(pp, kt)], tk_vtb.get(kt), tok_vones] \
                + list(deps_extra)
            if kt == 0:
                deps += [bankF[pvb + h2]]
            if h2 == 0:
                vs = H.vffn[:, kt, php, 0:128]
                ot = pvt[:, 0:512]
            else:
                vs = H.vffn[:, kt, php, 32:160]
                ot = pvt[:, 512:1024]
            b.t(deps, lambda e, kt=kt, vs=vs, ot=ot, h2=h2, php=php: e.matmul(
                ot, vs, H.pffn[:, h2, kt, :],
                start=(kt == 0), stop=(kt == NTB - 1)),
                inc=None if h2 == 0 else "tn")
        tk_pvkt[(pp, kt)] = ("tn", b.cnt["tn"])

    def norm_recips(pp):
        tok_pvl = tk_pvkt[(pp, NTB - 1)]
        pvt = H.ps[2 + pairno[pp] % 2]
        def _r1(e, pvt=pvt):
            with H.nc.allow_low_precision("softmax denom applied to bf16 attn"):
                return e.reciprocal(H.fsc[0:1, :], pvt[64:65, 0:512])

        def _r2(e, pvt=pvt):
            with H.nc.allow_low_precision("softmax denom applied to bf16 attn"):
                return e.reciprocal(H.fsc[32:33, :], pvt[32:33, 512:1024])
        tok_re = b.v([tok_pvl, tk_rope_prev[0]], _r1)
        tok_ro = b.v([tok_re], _r2)
        return (pp, tok_re, tok_ro)

    def norm_tail(pp, tok_re, tok_ro):
        pg, php = pp
        gsl = slice(pg * 512, (pg + 1) * 512)
        pvt = H.ps[2 + pairno[pp] % 2]
        pvb = 4 + 2 * (pairno[pp] % 2)
        tok_fe = b.t([tok_re, bankF[1]], lambda e: e.matmul(
            _bank(H, 1), H.onesrb[0:1, :], H.fsc[0:1, :],
            start=True, stop=True))
        tok_fo = b.t([tok_ro, bankF[3]], lambda e: e.matmul(
            _bank(H, 3), H.onesrb[32:33, :], H.fsc[32:33, :],
            start=True, stop=True))
        tok_c1 = b.v([tok_fe, tok_fo], lambda e: e.tensor_copy(
            H.fsc[:], _bank(H, 1)))
        bankF[1] = tok_c1
        tok_ae = b.v([tok_c1], lambda e, php=php, gsl=gsl, pvt=pvt:
                     e.tensor_tensor(H.attn[0:64, php, gsl],
                                     pvt[0:64, 0:512],
                                     H.fsc[0:64, :], ALU.mult))
        tok_c2 = b.v([tok_ae, tok_fo], lambda e: e.tensor_copy(
            H.fsc[:], _bank(H, 3)))
        bankF[3] = tok_c2
        tok_ao = b.v([tok_c2], lambda e, php=php, gsl=gsl, pvt=pvt:
                     e.tensor_tensor(H.attn[64:128, php, gsl],
                                     pvt[64:128, 512:1024],
                                     H.fsc[64:128, :], ALU.mult))
        bankF[pvb] = bankF[pvb + 1] = tok_ao
        tk_rope_prev[0] = tok_ao
        return tok_ao

    tk_rope_prev = [tk_rope_prev]
    tk_scolast = {}
    tk_vtb = {}
    pairno = {}
    pend = [None]
    vq = [(tb, dg) for tb in range(NTB) for dg in range(2)]
    vqi = [0]

    def vgroup(bk):
        tb, dg = vq[vqi[0]]
        vqi[0] += 1
        for ci in range(NCH):
            deps = [tok_wv, tok_vones, tok_vrow, bankF[bk]] if ci == 0 else []
            b.t(deps, lambda e, tb=tb, dg=dg, ci=ci, bk=bk: e.matmul(
                _bank(H, bk), xnp_tok(ci, 1 + tb * 128, 128),
                H.wvt[:, ci, dg * 512:(dg + 1) * 512],
                start=(ci == 0), stop=False), inc=None)
        b.t([], lambda e, dg=dg, bk=bk: e.matmul(
            _bank(H, bk), H.onesrb[0:1, :], H.vrow[0:1, dg * 512:(dg + 1) * 512],
            start=False, stop=True))
        tok_mm = ("tn", b.cnt["tn"])
        src = _bank(H, bk).rearrange("p (h x) -> p h x", h=4)
        tok_e1 = b.v([tok_mm], lambda e, tb=tb, dg=dg, src=src:
                     e.tensor_copy(H.vffn[:, tb, 4 * dg:4 * dg + 4, 0:64],
                                   src[:, :, 0:64]))
        tok_e2 = b.v([tok_e1], lambda e, tb=tb, dg=dg, src=src:
                     e.tensor_copy(H.vffn[:, tb, 4 * dg:4 * dg + 4, 96:160],
                                   src[:, :, 64:128]))
        bankF[bk] = tok_e2
        if dg == 1:
            tk_vtb[tb] = tok_e2

    npair = 0
    qproj(0)
    for g in range(NQG):
        for hp in range(NCH):
            pp = (g, hp)
            pairno[pp] = npair
            for kt in range(NTB):
                if kt == 6 and pend[0] is not None:
                    tok_attn = norm_tail(*pend[0])
                    pend[0] = None
                sb_e = kt % 2
                sb_o = 2 + kt % 2
                hf = (kt % 2) * 512
                deps_e = [tk_q[hp], tk_k0.get((hp, NQG - 1)), bankF[sb_e]]
                b.t(deps_e, lambda e, hp=hp, kt=kt, hf=hf: e.matmul(
                    H.psb[:, hf:hf + 512],
                    H.k_[0:64, hp, kt * 128:(kt + 1) * 128],
                    H.q0[0:64, hp, :], start=True, stop=True), inc=None)
                b.t([bankF[sb_o]], lambda e, hp=hp, kt=kt, hf=hf: e.matmul(
                    H.psb[:, 1024 + hf:1024 + hf + 512],
                    H.k_[64:128, hp, kt * 128:(kt + 1) * 128],
                    H.q0[64:128, hp, :], start=True, stop=True,
                    tile_position=(64, 0)), inc="tn")
                tok_sco = ("tn", b.cnt["tn"])
                if vqi[0] < len(vq) and npair < 2:
                    vgroup(6 + vqi[0] % 2)
                if prev is not None:
                    pv_kt(prev, kt)
                dep_pv = ([tk_pvkt.get((prev, kt))] if prev is not None
                          else [])
                src = bass.AP(H.psb, hf, [[2048, 128], [1024, 2], [1, 512]])
                tok_pe = b.s([tok_sco] + dep_pv,
                             lambda e, kt=kt, src=src: e.activation(
                                 H.pffn[:, :, kt, :], src,
                                 AF.Exp, bias=cells[:, 2:3],
                                 scale=cells[:, 1:2]))
                bankF[sb_e] = tok_pe
                bankF[sb_o] = tok_pe
                tk_ps_all[(pp, kt)] = tok_pe
                tk_scolast[hp] = tok_sco
            if hp == NCH - 1 and g + 1 < NQG:
                if prev is not None:
                    qproj(g + 1)
                    pend[0] = norm_recips(prev)
                else:
                    qproj(g + 1)
            elif prev is not None:
                pend[0] = norm_recips(prev)
            prev = pp
            npair += 1
    for kt in range(NTB):
        pv_kt(prev, kt)
        if kt == 6 and pend[0] is not None:
            tok_attn = norm_tail(*pend[0])
            pend[0] = None
    tok_attn = norm_tail(*norm_recips(prev))
    tk_rope_prev = tk_rope_prev[0]
    _fence(b)

    wf_src = H.d_wfb.ap().rearrange("(c p) j -> p c j", p=128)
    tok_wf = None
    b.dma("q", [tok_attn], "dd", lambda e: e.dma_start(H.ebsb[:], H.d_eb.ap()))
    tok_eb = b.dma("q", [], "dd", lambda e: e.dma_start(H.seln[:], H.d_seln.ap()))

    tk_wtg = {}
    tk_gpost = {}
    tok_g = None

    def gates_pre(g):
        gsl = slice(g * 512, (g + 1) * 512)
        for ci in range(NCH):
            deps = [tok_attn, tok_dd] if ci == 0 else []
            if ci == 0:
                deps.append(bankF[0])
            b.t(deps, lambda e, g=g, ci=ci, gsl=gsl: e.matmul(
                _bank(H, 0)[0:8, :], H.gw[:, ci * 8:(ci + 1) * 8],
                H.attn[:, ci, gsl], start=(ci == 0), stop=(ci == NCH - 1)),
                inc=None if ci < NCH - 1 else "tn")
        tok_gl = ("tn", b.cnt["tn"])
        tok_gb2 = b.v([tok_gl, tok_dy, tok_g], lambda e: e.tensor_scalar(
            H.tmpA[0:8, 0:512], _bank(H, 0)[0:8, :], H.gbcol[:], None,
            ALU.add))
        tk_t1 = {}
        for tq in range(4):
            b.t([tok_gb2], lambda e, tq=tq: e.transpose(
                _bank(H, 0)[:, tq * 8:(tq + 1) * 8],
                H.tmpA[0:8, tq * 128:(tq + 1) * 128],
                H.idf[0:8, 0:8]), inc=None if tq < 3 else "tn")
        tok_t1 = ("tn", b.cnt["tn"])
        gg_ = H.gsc
        tok_wm = None
        for tq in range(4):
            bsl = _bank(H, 0)[:, tq * 8:(tq + 1) * 8]
            tok_nm = b.v([tok_t1, tok_wm], lambda e, bsl=bsl: e.reduce_max(
                gg_[:, 8:9], bsl, AX.X, negate=True))
            tok_ex = b.s([tok_nm], lambda e, bsl=bsl: e.activation(
                gg_[:, 0:8], bsl, AF.Exp, bias=gg_[:, 8:9]))
            tok_s1 = b.v([tok_ex], lambda e: e.reduce_sum(
                gg_[:, 9:10], gg_[:, 0:8], AX.X))
            tok_r1 = b.v([tok_s1], lambda e: e.reciprocal(
                gg_[:, 10:11], gg_[:, 9:10]))
            tok_gt = b.v([tok_r1], lambda e: e.tensor_scalar(
                gg_[:, 12:20], gg_[:, 0:8], gg_[:, 10:11], None, ALU.mult))
            tok_m1 = b.v([tok_gt], lambda e: e.reduce_max(
                gg_[:, 8:9], gg_[:, 12:20], AX.X))
            tok_k1 = b.v([tok_m1], lambda e: e.tensor_scalar(
                gg_[:, 0:8], gg_[:, 12:20], gg_[:, 8:9], None, ALU.is_ge))
            tok_g2 = b.v([tok_k1], lambda e: e.scalar_tensor_tensor(
                gg_[:, 0:8], gg_[:, 0:8], -1e30, gg_[:, 12:20],
                ALU.mult, ALU.add))
            tok_m2 = b.v([tok_g2], lambda e: e.reduce_max(
                gg_[:, 9:10], gg_[:, 0:8], AX.X))
            tok_k2 = b.v([tok_m2], lambda e: e.tensor_scalar(
                gg_[:, 0:8], gg_[:, 12:20], gg_[:, 9:10], None, ALU.is_ge))
            tok_wm = b.v([tok_k2], lambda e, tq=tq: e.tensor_tensor(
                gg_[:, 24 + tq * 8:32 + tq * 8], gg_[:, 0:8],
                gg_[:, 12:20], ALU.mult))
        tk_gpost[g] = tok_wm

    def gates_post(g):
        gsl = slice(g * 512, (g + 1) * 512)
        gg_ = H.gsc
        for tq in range(4):
            b.t([tk_gpost[g]], lambda e, tq=tq: e.matmul(
                _bank(H, 0)[0:8, tq * 128:(tq + 1) * 128],
                gg_[:, 24 + tq * 8:32 + tq * 8], H.idf[:, 0:128],
                start=True, stop=True, is_transpose=True),
                inc=None if tq < 3 else "tn")
        tok_t2 = ("tn", b.cnt["tn"])
        tok_wt = b.v([tok_t2], lambda e, gsl=gsl: e.tensor_copy(
            H.wtbf[0:8, gsl], _bank(H, 0)[0:8, 0:512]))
        bankF[0] = tok_wt
        tk_wtg[g] = tok_wt

    gates_pre(0)
    gates_post(0)

    ew_slabs = [H.ewa, H.ewb, H.ewc, H.ewd]
    ew_sems = ["db", "dc", "da", "dm"]
    tk_slab = {}
    tk_slab_used = {}
    tk_iter = {}

    def ew_src(dc, q):
        return H.d_ewb.ap()[dc, q].rearrange("n d p j -> p (n d) j")

    NIT = NQG * 4
    ew_eng = ["y", "y", "q", "q"]
    for pre in range(3):
        pdout, pq = pre % NCH, 0
        tok_sl = b.dma(ew_eng[pre % 4], [tok_attn, tok_castall],
                       ew_sems[pre % 4],
                       lambda e, pdout=pdout: e.dma_start(
                           ew_slabs[pre % 4][:], ew_src(pdout, 0)))
        tk_slab[pre] = tok_sl

    tok_moe = None
    tk_moe = {}
    tk_x_prev = None

    tk_u = {}

    def ffnproj_mm(g, lo, hi):
        nonlocal tok_g
        gsl = slice(g * 512, (g + 1) * 512)
        for dout in range(lo, hi):
            bk = 1 + ((dout + g) % 3)
            for ci in range(NCH):
                deps = [tk_moe[(g, ci)]]
                if ci == 0:
                    deps += [tok_wf, bankF[bk]]
                b.t(deps, lambda e, dout=dout, ci=ci, gsl=gsl, bk=bk:
                    e.matmul(
                    _bank(H, bk), H.ffnw[:, ci, dout * 128:(dout + 1) * 128],
                    H.moe[:, ci, gsl], start=(ci == 0), stop=(ci == NCH - 1)),
                    inc=None if ci < NCH - 1 else "tn")
            tok_fp = ("tn", b.cnt["tn"])
            tok_u = b.v([tok_fp, tok_g], lambda e, dout=dout, g=g, bk=bk:
                        e.tensor_tensor(H.ustg[:, dout, :], _bank(H, bk),
                                        H.xnp[:, dout,
                                              1 + g * 512:1 + (g + 1) * 512],
                                        ALU.add))
            tk_u[dout] = tok_u
            bankF[bk] = tok_u

    def ffnproj_silu(g):
        nonlocal tok_g
        gsl = slice(g * 512, (g + 1) * 512)
        tok_ch7 = ("tn", b.cnt["tn"])
        for dout in range(NCH):
            tok_g = b.s([tk_u[dout], tok_ch7], lambda e, dout=dout, gsl=gsl:
                        e.activation(
                H.g_[:, dout, gsl], H.ustg[:, dout, :], AF.Silu,
                bias=H.biasP[:, 16 + dout:17 + dout]))

    for g in range(NQG):
        gsl = slice(g * 512, (g + 1) * 512)
        tk_wr = {}
        pend_fp = g - 1
        for n in range(NE):
            bk = n % 2
            dep_rp = [tk_wtg[g], bankF[bk], tok_eb]
            if g == 0 and n < 2:
                dep_rp.append(tok_attn)
            tok_rp = b.t(dep_rp, lambda e, n=n, gsl=gsl: e.matmul(
                _bank(H, bk), H.seln[0:8, n * 128:(n + 1) * 128],
                H.wtbf[0:8, gsl], start=True, stop=True))
            tok_we = b.v([tok_rp, tk_x_prev], lambda e, n=n, bk=bk: e.tensor_copy(
                H.wrep[:, n, :], _bank(H, bk)))
            bankF[bk] = tok_we
            tk_wr[n] = tok_we
        if pend_fp >= 0:
            ffnproj_mm(pend_fp, 4, NCH)
            ffnproj_silu(pend_fp)
        for h in range(2):
            for dq in range(2):
                it = (g * 2 + h) * 2 + dq
                xb_ = it % 2
                tk_x = {}
                for n in range(NE):
                    for dic in range(2):
                        ci = (2 * h + dq) * 2 + dic
                        tk_x[(n, dic)] = b.v(
                            [tk_wr[n], tk_iter.get(it - 2)],
                            lambda e, n=n, dic=dic, ci=ci, xb_=xb_, gsl=gsl:
                            e.tensor_tensor(H.xsl[:, xb_, n, dic, :],
                                            H.attn[:, ci, gsl],
                                            H.wrep[:, n, :], ALU.mult))
                tk_x_prev = tk_x[(NE - 1, 1)]
                q = 2 * h + dq
                for dout in range(NCH):
                    sidx = it * NCH + dout
                    si = sidx % 4
                    nsidx = sidx + 3
                    if nsidx < NIT * NCH:
                        nit, ndout = nsidx // NCH, nsidx % NCH
                        nq = (nit % 4)
                        nsi = nsidx % 4
                        dep_pf = [tk_slab_used.get(("mm", nsidx - 4))
                                  or tok_attn, tok_castall]
                        tok_pf = b.dma(ew_eng[nsi], dep_pf, ew_sems[nsi],
                                       lambda e, ndout=ndout, nq=nq, nsi=nsi:
                                       e.dma_start(ew_slabs[nsi][:],
                                                   ew_src(ndout, nq)))
                        tk_slab[nsidx] = tok_pf
                    if sidx == 6:
                        tok_wf = b.dma("q", [tok_castall], "de",
                                       lambda e: e.dma_start(H.ffnw[:],
                                                             wf_src))
                    bk = dout % 8
                    first = (h == 0 and dq == 0)
                    last_iter = (h == 1 and dq == 1)
                    for n in range(NE):
                        for dic in range(2):
                            deps = [tk_x[(n, dic)]]
                            if n == 0 and dic == 0:
                                deps += [tk_slab[sidx]]
                                if first:
                                    deps.append(bankF[bk])
                            is_last_mm = (n == NE - 1 and dic == 1)
                            b.t(deps, lambda e, si=si, n=n, dic=dic, xb_=xb_,
                                       bk=bk, st=(first and n == 0 and dic == 0):
                                e.matmul(
                                    _bank(H, bk), ew_slabs[si][:, n * 2 + dic, :],
                                    H.xsl[:, xb_, n, dic, :],
                                    start=st, stop=False),
                                inc="tn" if is_last_mm else None)
                    tok_grp = ("tn", b.cnt["tn"])
                    tk_slab_used[("mm", sidx)] = tok_grp
                    if last_iter:
                        tok_ebm = b.t([tok_eb], lambda e, dout=dout, gsl=gsl,
                                      bk=bk: e.matmul(
                            _bank(H, bk),
                            H.ebsb[:, dout * 128:(dout + 1) * 128],
                            H.wtbf[0:8, gsl], start=False, stop=True))
                        tok_moe = b.s([tok_ebm], lambda e, dout=dout, gsl=gsl,
                                      bk=bk: e.activation(
                            H.moe[:, dout, gsl], _bank(H, bk),
                            AF.Identity, bias=H.scal[:, 7:8]))
                        tk_moe[(g, dout)] = tok_moe
                        bankF[bk] = tok_moe
                tk_iter[it] = tok_grp
        if g + 1 < NQG:
            gates_pre(g + 1)
            ffnproj_mm(g, 0, 4)
            gates_post(g + 1)
        else:
            ffnproj_mm(g, 0, 4)
            ffnproj_mm(g, 4, NCH)
            ffnproj_silu(g)

    tok_mm31 = ("tn", b.cnt["tn"])
    ww_src = H.d_wwb.ap().rearrange("(c p) j -> p c j", p=128)
    tok_ww = b.dma("q", [tk_x_prev], "da",
               lambda e: e.dma_start(H.wwt[:], ww_src))
    tok_mk = b.dma("q", [tok_mm31], "dm",
                   lambda e: e.dma_start(H.mask[:], H.d_mask.ap()))

    wr_src = H.d_wrb.ap().rearrange("(c p) j -> p c j", p=128)
    tok_wr2 = b.dma("y", [tok_mm31], "dc",
                    lambda e: e.dma_start(H.wrt[:], wr_src))

    _fence(b)
    tok_jt = b.v([tok_lg, tok_dy, tok_g], lambda e: e.tensor_scalar(
        H.tmpA[:, 0:512], H.jtab[:], H.scal[:, 5:6], None, ALU.mult))
    tk_A = {}
    for i, dl in enumerate(DELTAS):
        tok_bd = b.v([tok_lg], lambda e, i=i, dl=dl: e.tensor_scalar(
            H.gsc[:, i:i + 1], H.scal[:, 5:6], float(dl - 1), None, ALU.mult))
        tok_a = b.s([tok_jt, tok_bd, tok_g], lambda e, i=i: e.activation(
            H.aslab[:, i, :], H.tmpA[:, 0:512], AF.Exp, bias=H.gsc[:, i:i + 1]))
        if i < 4:
            tok_a = b.v([tok_a, tok_mk], lambda e, i=i: e.tensor_tensor(
                H.aslab[:, i, :], H.aslab[:, i, :], H.mask[:, i, :], ALU.mult))
        tk_A[i] = tok_a

    tok_vl = None
    for tb in range(NTB):
        for dg in range(2):
            bk = (tb * 2 + dg) % 4
            for ci in range(NCH):
                deps = [tok_ww] if ci == 0 else []
                if ci == 0:
                    deps.append(bankF[bk])
                b.t(deps, lambda e, tb=tb, dg=dg, ci=ci, bk=bk: e.matmul(
                    _bank(H, bk), xnp_tok(ci, 1 + tb * 128, 128),
                    H.wwt[:, ci, dg * 512:(dg + 1) * 512],
                    start=(ci == 0), stop=(ci == NCH - 1)),
                    inc=None if ci < NCH - 1 else "tn")
            tok_mm = ("tn", b.cnt["tn"])
            tok_vl = b.v([tok_mm], lambda e, tb=tb, dg=dg, bk=bk: e.tensor_copy(
                H.vlin[:, tb, dg * 512:(dg + 1) * 512], _bank(H, bk)))
            bankF[bk] = tok_vl

    tk_store = {}
    nst = [0, 0]
    tk_xl = {}

    def emit_plin(g):
        tk_P = {}
        for tp in range(4 * g + 4):
            ai = 4 * g - tp + 3
            bk = tp % 4
            for ci in range(NCH):
                deps = [tok_vl] if ci == 0 else []
                if ci == 0:
                    deps.append(bankF[bk])
                b.t(deps, lambda e, tp=tp, ci=ci, g=g, bk=bk: e.matmul(
                    _bank(H, bk), xnp_tok(ci, tp * 128, 128),
                    xnp_tok(ci, 1 + g * 512, 512),
                    start=(ci == 0), stop=(ci == NCH - 1)),
                    inc=None if ci < NCH - 1 else "tn")
            tok_mm = ("tn", b.cnt["tn"])
            tk_P[tp] = b.v([tok_mm, tk_A[ai]],
                           lambda e, tp=tp, ai=ai, bk=bk: e.tensor_tensor(
                               H.plin[:, tp, :], _bank(H, bk),
                               H.aslab[:, ai, :], ALU.mult))
            bankF[bk] = tk_P[tp]
        return tk_P

    def emit_reads(g, tk_P):
        tk_rd = {}
        for ci in range(NCH):
            bk = 4 + (ci % 2)
            ntp = 4 * g + 4
            for tp in range(ntp):
                deps = [tk_P[tp]]
                if tp == 0:
                    deps.append(bankF[bk])
                b.t(deps, lambda e, ci=ci, tp=tp, bk=bk, ntp=ntp: e.matmul(
                    _bank(H, bk), H.vlin[:, tp, ci * 128:(ci + 1) * 128],
                    H.plin[:, tp, :], start=(tp == 0), stop=(tp == ntp - 1)),
                    inc=None if tp < ntp - 1 else "tn")
            tok_mm = ("tn", b.cnt["tn"])
            tk_rd[ci] = b.v([tok_mm], lambda e, ci=ci, bk=bk: e.tensor_copy(
                H.reads[:, ci, :], _bank(H, bk)))
            bankF[bk] = tk_rd[ci]
        return tk_rd

    tk_P = emit_plin(0)
    tk_rd = emit_reads(0, tk_P)
    for g in range(NQG):
        tk_sg = {}
        for co in range(NCH):
            bk = 6 + (co % 2)
            for ci in range(NCH):
                deps = [tok_wr2, tk_rd[ci]]
                if ci == 0:
                    deps.append(bankF[bk])
                b.t(deps, lambda e, co=co, ci=ci, bk=bk: e.matmul(
                    _bank(H, bk), H.wrt[:, ci, co * 128:(co + 1) * 128],
                    H.reads[:, ci, :], start=(ci == 0), stop=(ci == NCH - 1)),
                    inc=None if ci < NCH - 1 else "tn")
            tok_mm = ("tn", b.cnt["tn"])
            tok_sf = b.v([tok_mm, tok_alv, tk_A[15], tk_sg.get(co - 1)],
                         lambda e, co=co, g=g, bk=bk:
                         e.scalar_tensor_tensor(
                             H.tmpA[:, 0:512], _bank(H, bk), H.scal[:, 6:7],
                             H.xnp[:, co, 1 + g * 512:1 + (g + 1) * 512],
                             ALU.mult, ALU.add))
            bankF[bk] = tok_sf
            tk_sg[co] = b.v([tok_sf, tok_g], lambda e, co=co, g=g:
                            e.tensor_tensor(H.sgf[:, co, :], H.tmpA[:, 0:512],
                                            H.g_[:, co, g * 512:(g + 1) * 512],
                                            ALU.mult))
        if g + 1 < NQG:
            tk_P = emit_plin(g + 1)
        for tq in range(4):
            tb = g * 4 + tq
            dep_xl = [tk_store.get(tb - 4)]
            if tb < 4:
                dep_xl.append(tok_mm31)
            tk_xl[tb] = b.dma("q", dep_xl, "dl%d" % (tb % 4),
                              lambda e, tb=tb: e.dma_start(
                                  H.xtml[:, tb % 4, :],
                                  H.d_x.ap()[tb * 128:(tb + 1) * 128, :]))
        for tq in range(4):
            tb = g * 4 + tq
            par = tb % 2
            baseA = 4 if par == 0 else 6
            tok_xl = tk_xl[tb]
            tok_add = None
            for half in range(2):
                bk = baseA + half
                for c4 in range(4):
                    co = half * 4 + c4
                    deps = [tk_sg[co]]
                    if c4 == 0:
                        deps.append(bankF[bk])
                    b.t(deps, lambda e, co=co, tq=tq, bk=bk, c4=c4: e.matmul(
                        _bank(H, bk)[:, c4 * 128:(c4 + 1) * 128],
                        H.sgf[:, co, tq * 128:(tq + 1) * 128], H.idf[:],
                        start=True, stop=True, is_transpose=True),
                        inc=None if c4 < 3 else "tn")
                tok_tp = ("tn", b.cnt["tn"])
                dep_st = [tok_tp, tok_xl]
                if tb >= 2:
                    dep_st.append(tk_store.get(tb - 2))
                tok_add = b.v(dep_st, lambda e, half=half, par=par, bk=bk,
                              tb=tb: e.tensor_tensor(
                                  H.outtm[:, par, half * 512:(half + 1) * 512],
                                  _bank(H, bk), H.xtml[:, tb % 4,
                                  half * 512:(half + 1) * 512], ALU.add))
                bankF[bk] = tok_add
            tok_st = b.dma("y", [tok_add], "ds0" if par == 0 else "ds1",
                           lambda e, tb=tb, par=par: e.dma_start(
                               H.d_out.ap()[tb * 128:(tb + 1) * 128, :],
                               H.outtm[:, par, :]))
            nst[par] += 1
            tk_store[tb] = tok_st
        if g + 1 < NQG:
            tk_rd = emit_reads(g + 1, tk_P)

    b.wait("y", [("ds0", 16 * nst[0]), ("ds1", 16 * nst[1])])


def _host_prep(inputs):
    A = np.ascontiguousarray
    sh = {}
    sh["wqT"] = A(inputs["L0_ffn_q_w"].T)
    sh["wkT"] = A(inputs["L0_ffn_k_w"].T)
    sh["wvT"] = A(inputs["L0_ffn_v_w"].T)
    sh["wwT"] = A(inputs["L0_attn_W_write"].T)
    sh["wrT"] = A(inputs["L0_attn_W_read"].T)
    sh["wfT"] = A(inputs["L0_ffn_ffn_w"].T)
    ew = np.asarray(inputs["L0_ffn_expert_w"])
    sh["ewT"] = A(ew.reshape(NE, NE, 128, 4, 2, 128)
                  .transpose(1, 3, 0, 4, 5, 2))
    sh["eb"] = A(np.asarray(inputs["L0_ffn_expert_b"]))
    bp = np.zeros((128, 24), np.float32)
    bp[:, 0:8] = np.asarray(inputs["L0_ffn_q_b"]).reshape(8, 128).T
    bp[:, 8:16] = np.asarray(inputs["L0_ffn_k_b"]).reshape(8, 128).T
    bp[:, 16:24] = np.asarray(inputs["L0_ffn_ffn_b"]).reshape(8, 128).T
    sh["biasP"] = bp
    sh["gbcol"] = A(np.asarray(inputs["L0_ffn_gate_b"]).reshape(8, 1))
    sh["vrow"] = A(np.asarray(inputs["L0_ffn_v_b"]).reshape(1, DM))
    gw = np.asarray(inputs["L0_ffn_gate_w"])
    sh["gwT"] = A(gw.T.reshape(NE, 128, NE).transpose(1, 0, 2).reshape(128, 64))
    inv = (10000.0 ** (-np.arange(0, 64, 2, dtype=np.float64) / 64.0))
    ang = np.outer(np.arange(T, dtype=np.float64), inv)
    cos64 = np.cos(np.concatenate([ang, ang], axis=1)).T
    sin64 = np.sin(np.concatenate([ang, ang], axis=1)).T
    sh["cosT"] = np.vstack([cos64, cos64]).astype(np.float32)
    sh["sinT"] = np.vstack([sin64, sin64]).astype(np.float32)
    P = np.zeros((128, 128), np.float32)
    for blk in range(2):
        o = blk * 64
        for j in range(32):
            P[o + j + 32, o + j] = -1.0
            P[o + j, o + j + 32] = 1.0
    sh["prot"] = P
    sh["idf"] = np.eye(128, dtype=np.float32)
    sh["onesr"] = np.ones((64, 128), np.float32)
    sh["onesc"] = np.ones((128, 1), np.float32)
    sh["onesrb"] = np.ones((64, 128), np.float32)
    seln = np.zeros((NE, DM), np.float32)
    for n in range(NE):
        seln[n, n * 128:(n + 1) * 128] = 1.0
    sh["seln"] = seln
    jj = (np.arange(512)[None, :] - np.arange(128)[:, None]).astype(np.float32)
    sh["jtab"] = A(jj)
    masks = np.zeros((128, 4, 512), np.float32)
    for i in range(4):
        dl = DELTAS[i]
        masks[:, i, :] = ((np.arange(512)[None, :] - np.arange(128)[:, None]
                           + dl) > 0).astype(np.float32)
    sh["maskT"] = masks
    sh["decay"] = np.asarray(inputs["L0_attn_decay"],
                             np.float32).reshape(1, 1)
    sh["log_alpha"] = np.asarray(inputs["L0_attn_log_alpha"],
                                 np.float32).reshape(1, 1)
    return sh


def kernel(**inputs):
    sh = _host_prep(inputs)
    x = np.asarray(inputs["L0_attn_out"], np.float32)
    nc, H = _build_nc()
    in_maps = []
    for core in range(8):
        m = dict(sh)
        m["x"] = np.ascontiguousarray(x[core])
        in_maps.append(m)
    res = run_bass_kernel_spmd(nc, in_maps, core_ids=list(range(8)))
    out = np.stack([r["out"] for r in res.results], axis=0)
    return out
